# version 31
# baseline (speedup 1.0000x reference)
import numpy as np
from contextlib import ExitStack

import concourse.bass as bass
import concourse.mybir as mybir
from concourse.bass_utils import run_bass_kernel_spmd

F32 = mybir.dt.float32
BF16 = mybir.dt.bfloat16
I32 = mybir.dt.int32
AF = mybir.ActivationFunctionType
ALU = mybir.AluOpType
AX = mybir.AxisListType

D = 1024
NT = 4096
H = 8
DK = 128
CH = 128
GC = 8
NG = 4
GT = GC * CH
KC = 8
IF = 3072
IC = 24
LN16 = float(np.log(16.0))
QM_ACT = 32767.0
QM_W = 7.0
EPS_LN = 1e-5
EPS_RMS = 1e-5

ENGS = ("sync", "scalar", "vector", "tensor", "gpsimd")

_DEBUG_TAPS = False


class Ctr:
    __slots__ = ("sem", "step", "n")

    def __init__(self, sem, step=1):
        self.sem = sem
        self.step = step
        self.n = 0

    def inc(self, inst):
        self.n += 1
        if inst:
            inst.then_inc(self.sem, self.step)
        return self.n * self.step

    @property
    def val(self):
        return self.n * self.step


class NS:
    pass


def _wt(X, ctr, val):
    if X and val > 0:
        X.wait_ge(ctr.sem, val)


def build_program():
    nc = bass.Bass("TRN2", target_bir_lowering=False, debug=False)

    xT = nc.dram_tensor("xT", [D, NT], F32, kind="ExternalInput").ap()
    c_row = nc.dram_tensor("c_row", [1, D], F32, kind="ExternalInput").ap()
    wqT = nc.dram_tensor("wqT", [D, D], F32, kind="ExternalInput").ap()
    wkT = nc.dram_tensor("wkT", [D, D], F32, kind="ExternalInput").ap()
    wvT = nc.dram_tensor("wvT", [D, D], F32, kind="ExternalInput").ap()
    woT = nc.dram_tensor("woT", [D, D], F32, kind="ExternalInput").ap()
    fmqT = nc.dram_tensor("fmqT", [DK, DK], F32, kind="ExternalInput").ap()
    fmkT = nc.dram_tensor("fmkT", [DK, DK], F32, kind="ExternalInput").ap()
    fmq_b = nc.dram_tensor("fmq_b", [DK, 1], F32, kind="ExternalInput").ap()
    fmk_b = nc.dram_tensor("fmk_b", [DK, 1], F32, kind="ExternalInput").ap()
    fmk_b_row = nc.dram_tensor("fmk_b_row", [1, DK], F32, kind="ExternalInput").ap()
    rms_row = nc.dram_tensor("rms_row", [1, DK], F32, kind="ExternalInput").ap()
    wmod = nc.dram_tensor("wmod", [D, 6 * D], F32, kind="ExternalInput").ap()
    w_up = nc.dram_tensor("w_up", [IF, D], F32, kind="ExternalInput").ap()
    w_gate = nc.dram_tensor("w_gate", [IF, D], F32, kind="ExternalInput").ap()
    w_down = nc.dram_tensor("w_down", [D, IF], F32, kind="ExternalInput").ap()
    out_fm = nc.dram_tensor("out_fm", [D, NT], F32, kind="ExternalOutput").ap()

    dtn = nc.dram_tensor
    sc_q = dtn("sc_q", [H, 128, KC * 128], BF16).ap()
    sc_k = dtn("sc_k", [H, 128, KC * 128], BF16).ap()
    sc_v = dtn("sc_v", [2, 128, KC * 512], BF16).ap()
    sc_o = dtn("sc_o", [KC, 128, H * 128], BF16).ap()
    sc_up = dtn("sc_up", [IC, 128, KC * 128], BF16).ap()
    sc_gt = dtn("sc_gt", [IC, 128, KC * 128], BF16).ap()
    sc_dn = dtn("sc_dn", [KC, 128, IC * 128], BF16).ap()
    sp_gate = dtn("sp_gate", [IC, 128, NT], BF16).ap()
    sp_up = dtn("sp_up", [IC, 128, NT], BF16).ap()
    sp_p = dtn("sp_p", [IC, 128, NT], BF16).ap()
    sp_o = dtn("sp_o", [IC, 128, NT], BF16).ap()
    sp_x2 = dtn("sp_x2", [KC, 128, NT], BF16).ap()
    sp_h2 = dtn("sp_h2", [KC, 128, NT], BF16).ap()
    sp_S = dtn("sp_S", [H, 128, 256], BF16).ap()
    ar_in = {k: dtn("ar%s_in" % k, [1, 2], F32).ap() for k in ("1", "3", "24", "5")}
    ar_out = {k: dtn("ar%s_out" % k, [1, 2], F32).ap() for k in ("1", "3", "24", "5")}

    taps = {}
    if _DEBUG_TAPS:
        taps["tap_mod"] = dtn("tap_mod", [1, 6 * D], BF16, kind="ExternalOutput").ap()
        taps["tap_h"] = dtn("tap_h", [128, KC * NT], BF16, kind="ExternalOutput").ap()
        taps["tap_x2"] = dtn("tap_x2", [128, KC * NT], BF16, kind="ExternalOutput").ap()
        taps["tap_wup"] = dtn("tap_wup", [128, KC * 128], BF16, kind="ExternalOutput").ap()
        taps["tap_xhat"] = dtn("tap_xhat", [128, KC * NT], BF16, kind="ExternalOutput").ap()
        taps["tap_scal"] = dtn("tap_scal", [128, 64], F32, kind="ExternalOutput").ap()
        taps["tap_ong"] = dtn("tap_ong", [128, H * GT], BF16, kind="ExternalOutput").ap()
        taps["tap_vg"] = dtn("tap_vg", [128, 8 * D], BF16, kind="ExternalOutput").ap()
        taps["tap_qm"] = dtn("tap_qm", [128, 2 * GT], BF16, kind="ExternalOutput").ap()
        taps["tap_km"] = dtn("tap_km", [128, 2 * GT], BF16, kind="ExternalOutput").ap()
        taps["tap_ktok"] = dtn("tap_ktok", [128, 2 * GT], BF16, kind="ExternalOutput").ap()
        taps["tap_Ssb"] = dtn("tap_Ssb", [128, 256], BF16, kind="ExternalOutput").ap()
        taps["tap_mprt"] = dtn("tap_mprt", [1, 96], F32, kind="ExternalOutput").ap()
        taps["tap_mrow"] = dtn("tap_mrow", [1, 16], F32, kind="ExternalOutput").ap()
        taps["tap_o4"] = dtn("tap_o4", [128, 512], BF16, kind="ExternalOutput").ap()
        taps["tap_o2"] = dtn("tap_o2", [128, 512], BF16, kind="ExternalOutput").ap()
        taps["tap_rstd"] = dtn("tap_rstd", [1, 512], F32, kind="ExternalOutput").ap()

    C = NS()
    C.qs = Ctr(nc.alloc_semaphore(name="qs"), 16)
    C.qa = Ctr(nc.alloc_semaphore(name="qa"), 16)
    C.qg = Ctr(nc.alloc_semaphore(name="qg"), 16)
    C.qw = Ctr(nc.alloc_semaphore(name="qw"), 16)
    C.qs2 = Ctr(nc.alloc_semaphore(name="qs2"), 16)
    C.qw2 = Ctr(nc.alloc_semaphore(name="qw2"), 16)
    C.cv = Ctr(nc.alloc_semaphore(name="cv"))
    C.cs = Ctr(nc.alloc_semaphore(name="cs"))
    C.ct = Ctr(nc.alloc_semaphore(name="ct"))
    C.cg = Ctr(nc.alloc_semaphore(name="cg"))
    CTRS = {k: getattr(C, k) for k in ("qs", "qa", "qg", "qw", "qs2", "qw2",
                                       "cv", "cs", "ct", "cg")}

    def run_walks(blk, fn):
        base = {k: c.n for k, c in CTRS.items()}
        ends = []
        for me in ENGS:
            for k, c in CTRS.items():
                c.n = base[k]
            getattr(blk, me)(lambda E, me=me: fn(me, E))
            ends.append({k: c.n for k, c in CTRS.items()})
        assert all(e == ends[0] for e in ends), ("walk mismatch", ends)

    def engs(me, E):
        return (E if me == "sync" else None, E if me == "scalar" else None,
                E if me == "vector" else None, E if me == "tensor" else None,
                E if me == "gpsimd" else None)

    glob = ExitStack()
    gsb = lambda n, s, d: glob.enter_context(nc.sbuf_tensor(n, s, d))
    maskT = gsb("maskT", [128, 128], BF16)
    ident_bf = gsb("ident_bf", [128, 128], BF16)
    onesc_bf = gsb("onesc_bf", [128, 1], BF16)
    ones11_bf = gsb("ones11_bf", [1, 1], BF16)
    ones11_f = gsb("ones11_f", [1, 1], F32)
    onesr_f = gsb("onesr_f", [1, 128], F32)
    ones1r_bf = gsb("ones1r_bf", [1, 128], BF16)
    onesr512_bf = gsb("onesr512_bf", [1, 512], BF16)
    cols = gsb("cols", [128, 64], F32)
    mrow = gsb("mrow", [1, 16], F32)
    vpadA = gsb("vpadA", [1, 64], F32)
    vpadB = gsb("vpadB", [1, 64], F32)
    epsln_sb = gsb("epsln_sb", [1, 1], F32)
    epsrms_sb = gsb("epsrms_sb", [1, 1], F32)
    eps64_sb = gsb("eps64_sb", [1, 1], F32)
    nl2_sb = gsb("nl2_sb", [1, 1], F32)
    sclr = gsb("sclr", [1, 32], F32)
    mprt = gsb("mprt", [1, 96], F32)
    mcols = gsb("mcols", [128, 96], F32)
    mcol1 = gsb("mcol1", [128, 1], F32)
    mcol1b = gsb("mcol1b", [128, 1], BF16)

    ab = ExitStack()
    absb = lambda n, s, d: ab.enter_context(nc.sbuf_tensor(n, s, d))
    stream = absb("stream", [128, KC * NT], BF16)
    arow = absb("arow", [1, NT], BF16)
    brow = absb("brow", [1, NT], BF16)
    gam_a = absb("gam_a", [1, D], BF16)
    gam_b = absb("gam_b", [1, D], BF16)
    sft_a = absb("sft_a", [1, D], BF16)
    sft_b = absb("sft_b", [1, D], BF16)
    seg1 = absb("seg1", [1, GT], F32)
    seg2 = absb("seg2", [1, GT], F32)
    segt = absb("segt", [1, GT], F32)
    segu = absb("segu", [1, GT], F32)
    fmqT_sb = absb("fmqT_sb", [128, 128], BF16)
    fmkT_sb = absb("fmkT_sb", [128, 128], BF16)
    fmkbr_bf = absb("fmkbr_bf", [1, 128], BF16)
    rmsr_bf = absb("rmsr_bf", [1, 128], BF16)
    fmqb_sb = absb("fmqb_sb", [128, 1], F32)
    fmkb_sb = absb("fmkb_sb", [128, 1], F32)

    def STRM(m, a, b):
        return stream[:, m * NT + a: m * NT + b]

    def vpads(V, n=2):
        r = 0
        for _ in range(n):
            r = C.cv.inc(V and V.tensor_copy(vpadB[:], vpadA[:]))
        return r

    def finalize_and_allreduce(me, E, pairs, key, wait_cg=0, cols_src=None):
        Y, S, V, T, G = engs(me, E)
        _wt(V, C.cg, wait_cg)
        if cols_src is not None:
            prow, wait_ct0 = cols_src
            for (a, b), slot in pairs:
                C.cv.inc(V and V.tensor_reduce(mcol1[:], mcols[:, a:b], AX.X,
                                               ALU.max))
                vpads(V)
                C.cv.inc(V and V.tensor_copy(mcol1b[:], mcol1[:]))
                nvp = vpads(V)
                _wt(T, C.cv, nvp)
                _wt(T, C.cv, wait_ct0)
                ntp = C.ct.inc(T and T.transpose(prow[:], mcol1b[:], ident_bf[:]))
                _wt(V, C.ct, ntp)
                C.cv.inc(V and V.tensor_reduce(mrow[0:1, slot:slot + 1],
                                               prow[0:1, 0:128], AX.X, ALU.max))
                wait_ct0 = vpads(V)
        else:
            for (a, b), slot in pairs:
                C.cv.inc(V and V.tensor_reduce(mrow[0:1, slot:slot + 1],
                                               mprt[0:1, a:b], AX.X, ALU.max))
        nv = vpads(V)
        lo = min(s for _, s in pairs)
        hi = max(s for _, s in pairs) + 1
        _wt(G, C.cv, nv)
        ng1 = C.qg.inc(G and G.dma_start(out=ar_in[key][0:1, 0:hi - lo],
                                         in_=mrow[0:1, lo:hi]))
        _wt(G, C.qg, ng1)
        ngc = C.cg.inc(G and G.collective_compute(
            "AllReduce", ALU.max, [list(range(8))],
            ins=[ar_in[key][0:1, 0:hi - lo].opt()],
            outs=[ar_out[key][0:1, 0:hi - lo].opt()]))
        _wt(G, C.cg, ngc)
        ng2 = C.qg.inc(G and G.dma_start(out=mrow[0:1, 8 + lo:8 + hi],
                                         in_=ar_out[key][0:1, 0:hi - lo]))
        return ng2

    def make_scale_cols(me, E, gslot, f_slot, s_slot, pcolp, ncol_base, qm=QM_ACT,
                        pcol_free=0):
        Y, S, V, T, G = engs(me, E)
        V and V.tensor_scalar(sclr[0:1, s_slot:s_slot + 1],
                              mrow[0:1, 8 + gslot:9 + gslot],
                              1.0 / qm, 1e-12, ALU.mult, ALU.max)
        vpads(V)
        C.cv.inc(V and V.reciprocal(sclr[0:1, f_slot:f_slot + 1],
                                    sclr[0:1, s_slot:s_slot + 1]))
        nv = vpads(V)
        _wt(T, C.cv, nv)
        _wt(T, C.cv, pcol_free)
        C.ct.inc(T and T.matmul(pcolp[:, 0:1], onesr_f[:],
                                sclr[0:1, f_slot:f_slot + 1], start=True, stop=True,
                                skip_group_check=True))
        nt = C.ct.inc(T and T.matmul(pcolp[:, 1:2], onesr_f[:],
                                     sclr[0:1, s_slot:s_slot + 1], start=True, stop=True,
                                     skip_group_check=True))
        _wt(V, C.ct, nt)
        nv2 = C.cv.inc(V and V.tensor_copy(cols[:, f_slot:f_slot + 2], pcolp[:, 0:2]))
        return nv2

    a_ = ExitStack()
    asb = lambda n, s, d: a_.enter_context(nc.sbuf_tensor(n, s, d))
    apsf = lambda n, s: a_.enter_context(nc.psum_tensor(n, s, F32))
    c_sb = asb("c_sb", [1, D], F32)
    cs_row = asb("cs_row", [1, D], BF16)
    ccols = asb("ccols", [128, 8], BF16)
    wmod_bf = [asb("wmod_bf%d" % i, [128, 3 * D // 2], BF16) for i in range(2)]
    mod_row = asb("mod_row", [1, 6 * D], BF16)
    garow = asb("garow", [1, D], BF16)
    gbrow = asb("gbrow", [1, D], BF16)
    rowt1 = asb("rowt1", [1, D], BF16)
    rowt2 = asb("rowt2", [1, D], BF16)
    xsq_big = asb("xsq_big", [128, 2048], BF16)
    tmp_a = asb("tmp_a", [128, 512], BF16)
    wnat = [asb("wnat%d" % i, [128, IF // 2], F32) for i in range(4)]
    wsc = asb("wsc", [128, 96], F32)
    wfc = asb("wfc", [128, 96], F32)
    wqi = asb("wqi", [128, IF], I32)
    wdq = [asb("wdq%d" % i, [128, IF], BF16) for i in range(2)]
    wstg = [asb("wstg%d" % i, [128, 12 * 128], BF16) for i in range(2)]

    pmod = apsf("pmod", [1, 512])
    pmisc = apsf("pmisc", [128, 64])
    ps1 = apsf("ps1", [1, 512])
    ps2 = apsf("ps2", [1, 512])
    pbc = [apsf("pbc%d" % i, [128, 512]) for i in range(2)]
    ptr = [a_.enter_context(nc.psum_tensor("ptr%d" % i, [128, 128], BF16))
           for i in range(2)]

    def phaseA(me, E):
        Y, S, V, T, G = engs(me, E)

        C.qs.inc(Y and Y.dma_start(out=c_sb[:], in_=c_row))
        C.qs.inc(Y and Y.dma_start(out=fmqb_sb[:], in_=fmq_b))
        nq0 = C.qs.inc(Y and Y.dma_start(out=fmkb_sb[:], in_=fmk_b))
        C.qg.inc(G and G.dma_start(out=fmkbr_bf[:], in_=fmk_b_row))
        C.qg.inc(G and G.dma_start(out=rmsr_bf[:], in_=rms_row))
        C.qg.inc(G and G.dma_start(out=fmqT_sb[:], in_=fmqT))
        C.qg.inc(G and G.dma_start(out=fmkT_sb[:], in_=fmkT))
        G and G.memset(maskT[:], 0.0)
        G and G.affine_select(maskT[:], maskT[:], [[-1, 128]], ALU.is_gt, 1.0,
                              base=0, channel_multiplier=1)
        G and G.memset(ident_bf[:], 1.0)
        ncst = C.cg.inc(G and G.affine_select(ident_bf[:], ident_bf[:], [[-1, 128]],
                                              ALU.is_equal, 0.0, base=0,
                                              channel_multiplier=1))
        V and V.memset(vpadA[:], 0.0)
        V and V.memset(vpadB[:], 0.0)
        V and V.memset(epsln_sb[:], EPS_LN)
        V and V.memset(epsrms_sb[:], EPS_RMS)
        V and V.memset(eps64_sb[:], EPS_RMS * 2.0 ** -64)
        V and V.memset(nl2_sb[:], -32.0 * float(np.log(2.0)))
        V and V.memset(onesc_bf[:], 1.0)
        V and V.memset(ones11_bf[:], 1.0)
        V and V.memset(ones11_f[:], 1.0)
        V and V.memset(onesr_f[:], 1.0)
        V and V.memset(ones1r_bf[:], 1.0)
        nv0 = C.cv.inc(V and V.memset(onesr512_bf[:], 1.0))

        _wt(S, C.qs, nq0)
        ns_silu = C.cs.inc(S and S.activation(cs_row[:], c_sb[:], AF.Silu))
        _wt(T, C.cs, ns_silu)
        _wt(T, C.cv, nv0)
        nt = 0
        for k in range(KC):
            nt = C.ct.inc(T and T.matmul(pmisc[:, k:k + 1],
                                         cs_row[0:1, 128 * k:128 * (k + 1)],
                                         ones11_bf[:], start=True, stop=True,
                                         skip_group_check=True))
        _wt(V, C.ct, nt)
        C.cv.inc(V and V.tensor_copy(ccols[:], pmisc[:, 0:8]))
        nv_cc = vpads(V)
        nv_mod = C.cv.inc(V and V.memset(mod_row[:], 0.0))
        wmb_done = [0, 0]
        for k in range(KC):
            for hf in range(4):
                pi = 4 * k + hf
                pb = pi % 2
                _wt(G, C.ct, wmb_done[pb])
                ngw = C.qg.inc(G and G.dma_start(
                    out=wmod_bf[pb][:],
                    in_=wmod[128 * k:128 * (k + 1), 1536 * hf:1536 * (hf + 1)]))
                _wt(T, C.qg, ngw)
                _wt(T, C.cv, nv_cc)
                for mt in range(3):
                    mtg = 3 * hf + mt
                    _wt(T, C.cv, nv_mod)
                    ntm = C.ct.inc(T and T.matmul(
                        pmod[:], ccols[:, k:k + 1],
                        wmod_bf[pb][:, 512 * mt:512 * (mt + 1)],
                        start=True, stop=True, skip_group_check=True))
                    wmb_done[pb] = ntm
                    _wt(V, C.ct, ntm)
                    nv_mod = C.cv.inc(V and V.tensor_tensor(
                        mod_row[0:1, 512 * mtg:512 * (mtg + 1)],
                        mod_row[0:1, 512 * mtg:512 * (mtg + 1)], pmod[:], ALU.add))
        nx = C.qg.inc(G and G.dma_start(
            out=stream[:].rearrange("p (m t) -> p m t", t=NT),
            in_=xT.rearrange("(m p) t -> p m t", p=128)))
        for h in range(H):
            C.qg.inc(G and G.dma_start(
                out=sc_q[h].rearrange("p (k f) -> p k f", f=128),
                in_=wqT.rearrange("(k p) (h f) -> h p k f", p=128, f=128)[h]))
            C.qg.inc(G and G.dma_start(
                out=sc_k[h].rearrange("p (k f) -> p k f", f=128),
                in_=wkT.rearrange("(k p) (h f) -> h p k f", p=128, f=128)[h]))
        for b in range(2):
            C.qg.inc(G and G.dma_start(
                out=sc_v[b].rearrange("p (k f) -> p k f", f=512),
                in_=wvT.rearrange("(k p) (b f) -> b p k f", p=128, f=512)[b]))
        for m in range(KC):
            C.qg.inc(G and G.dma_start(
                out=sc_o[m].rearrange("p (h f) -> p h f", f=128),
                in_=woT.rearrange("(h p) (m f) -> m p h f", p=128, f=128)[m]))

        if _DEBUG_TAPS:
            _wt(Y, C.cv, nv_mod)
            C.qs.inc(Y and Y.dma_start(out=taps["tap_mod"], in_=mod_row[:]))

        sa_s = mod_row[0:1, 0 * D:1 * D]
        ca_s = mod_row[0:1, 1 * D:2 * D]
        ga_s = mod_row[0:1, 2 * D:3 * D]
        sb_s = mod_row[0:1, 3 * D:4 * D]
        cb_s = mod_row[0:1, 4 * D:5 * D]
        gb_s = mod_row[0:1, 5 * D:6 * D]
        V and V.tensor_scalar(gam_a[:], ca_s, 1.0, None, ALU.add)
        V and V.tensor_scalar(gam_b[:], cb_s, 1.0, None, ALU.add)
        V and V.tensor_copy(sft_a[:], sa_s)
        V and V.tensor_copy(sft_b[:], sb_s)
        for src, dstrow in ((ga_s, garow), (gb_s, gbrow)):
            V and V.tensor_scalar(rowt1[:], src, -1.0, None, ALU.mult)
            V and V.tensor_tensor(rowt2[:], rowt1[:], src, ALU.max)
            V and V.tensor_scalar(dstrow[:], rowt2[:], 0.25, None, ALU.max)
            V and V.tensor_scalar(rowt1[:], src, 0.0, None, ALU.is_ge)
            V and V.tensor_scalar(rowt2[:], rowt1[:], 2.0, 1.0, ALU.mult, ALU.subtract)
            V and V.tensor_tensor(dstrow[:], rowt2[:], dstrow[:], ALU.mult)
        V and V.tensor_scalar(cols[:, 28:29], fmqb_sb[:], -LN16, None, ALU.add)
        V and V.tensor_scalar(cols[:, 29:30], fmqb_sb[:], -1.0, -LN16, ALU.mult, ALU.add)
        V and V.tensor_copy(cols[:, 30:31], fmkb_sb[:])
        nv_pp = C.cv.inc(V and V.tensor_scalar(cols[:, 31:32], fmkb_sb[:], -1.0, None,
                                               ALU.mult))
        _wt(T, C.cv, nv_pp)
        nt_g = 0
        for k in range(KC):
            C.ct.inc(T and T.matmul(pmisc[:, 16 + k:17 + k],
                                    garow[0:1, 128 * k:128 * (k + 1)], ones11_bf[:],
                                    start=True, stop=True, skip_group_check=True))
            nt_g = C.ct.inc(T and T.matmul(pmisc[:, 24 + k:25 + k],
                                           gbrow[0:1, 128 * k:128 * (k + 1)], ones11_bf[:],
                                           start=True, stop=True, skip_group_check=True))
        _wt(V, C.ct, nt_g)
        nv_gc0 = C.cv.inc(V and V.tensor_copy(cols[:, 0:16], pmisc[:, 16:32]))
        _wt(T, C.cv, nv_gc0)
        nt_g2 = 0
        for vi, row in enumerate((gam_a, sft_a, gam_b, sft_b)):
            for k in range(KC):
                nt_g2 = C.ct.inc(T and T.matmul(
                    pmisc[:, 8 * vi + k:8 * vi + k + 1],
                    row[0:1, 128 * k:128 * (k + 1)], ones11_bf[:],
                    start=True, stop=True, skip_group_check=True))
        _wt(V, C.ct, nt_g2)
        C.cv.inc(V and V.tensor_copy(cols[:, 32:64], pmisc[:, 0:32]))

        _wt(T, C.cg, ncst)
        wstg_dma = [0, 0]
        wnat_free = [0, 0, 0, 0]
        wdq_used = [0, 0]
        ptr_free = [0, 0]
        jobs = []
        for wsrc, scdst, ntile, width in ((w_up, sc_up, IC, D), (w_gate, sc_gt, IC, D),
                                          (w_down, sc_dn, KC, IF)):
            for it in range(ntile):
                if width <= IF // 2:
                    jobs.append((wsrc, scdst, it, width, 0))
                else:
                    jobs.append((wsrc, scdst, it, width // 2, 0))
                    jobs.append((wsrc, scdst, it, width // 2, 1))
        for ji, (wsrc, scdst, it, width, hv) in enumerate(jobs):
            if True:
                nblk = width // 32
                nkc = width // 128
                bi = ji % 4
                bq = ji % 2
                buf = wnat[bi]
                LE, lq = Y, C.qs
                _wt(LE, C.cv, wnat_free[bi])
                nload = lq.inc(LE and LE.dma_start(
                    out=buf[:, 0:width],
                    in_=wsrc[128 * it:128 * (it + 1),
                             hv * width:(hv + 1) * width]))
                _wt(V, lq, nload)
                w3 = buf[:, 0:width].rearrange("p (b c) -> p b c", c=32)
                C.cv.inc(V and V.tensor_reduce(
                    wsc[:, 0:nblk].unsqueeze(2), w3, AX.X, ALU.max,
                    apply_absolute_value=True))
                vpads(V)
                V and V.tensor_scalar(wsc[:, 0:nblk], wsc[:, 0:nblk], 1.0 / QM_W, 1e-12,
                                      ALU.mult, ALU.max)
                vpads(V)
                V and V.reciprocal(wfc[:, 0:nblk], wsc[:, 0:nblk])
                vpads(V)
                fb3 = wfc[:, 0:nblk].unsqueeze(2).broadcast_to([128, nblk, 32])
                sb3 = wsc[:, 0:nblk].unsqueeze(2).broadcast_to([128, nblk, 32])
                _wt(V, C.ct, wdq_used[bq])
                V and V.tensor_tensor(
                    wqi[:, 0:width].rearrange("p (b c) -> p b c", c=32), w3, fb3,
                    ALU.mult)
                nv_dq = C.cv.inc(V and V.tensor_tensor(
                    wdq[bq][:, 0:width].rearrange("p (b c) -> p b c", c=32),
                    wqi[:, 0:width].rearrange("p (b c) -> p b c", c=32), sb3, ALU.mult))
                wnat_free[bi] = nv_dq
                _wt(T, C.cv, nv_dq)
                _wt(T, C.qw2, wstg_dma[bq])
                nstg = 0
                ntt_last = 0
                for k in range(nkc):
                    _wt(T, C.cv, ptr_free[k % 2])
                    ntt_last = C.ct.inc(T and T.transpose(
                        ptr[k % 2][:], wdq[bq][:, 128 * k:128 * (k + 1)], ident_bf[:]))
                    _wt(V, C.ct, ntt_last)
                    nstg = C.cv.inc(V and V.tensor_copy(
                        wstg[bq][:, 128 * k:128 * (k + 1)], ptr[k % 2][:]))
                    ptr_free[k % 2] = nstg
                wdq_used[bq] = ntt_last
                WE, wq = S, C.qw2
                _wt(WE, C.cv, nstg)
                wstg_dma[bq] = wq.inc(WE and WE.dma_start(
                    out=scdst[it][:, hv * nkc * 128:(hv + 1) * nkc * 128],
                    in_=wstg[bq][:, 0:nkc * 128]))
                if _DEBUG_TAPS and wi == 0 and it == 0:
                    C.qs.inc(Y and Y.dma_start(out=taps["tap_wup"],
                                               in_=wstg[bq][:, 0:nkc * 128]))

        _wt(T, C.qg, nx)
        _wt(S, C.qg, nx)
        sq_used = 0
        for g in range(NG):
            t0 = g * GT
            nv_seg_prev = 0
            for tt in range(2):
                ta = t0 + 512 * tt
                strm3 = stream[:].rearrange("p (m t) -> p m t", t=NT)[:, :, ta:ta + 512]
                nt_st = 0
                for hq in range(2):
                    _wt(S, C.ct, sq_used)
                    ns_sq = C.cs.inc(S and S.activation(
                        xsq_big[:].rearrange("p (m t) -> p m t", t=512),
                        strm3[:, 4 * hq:4 * (hq + 1), :], AF.Square))
                    for mq in range(4):
                        m = 4 * hq + mq
                        _wt(T, C.cs, ns_sq)
                        if m == 0:
                            _wt(T, C.cv, nv_seg_prev)
                        C.ct.inc(T and T.matmul(ps1[:], onesc_bf[:],
                                                STRM(m, ta, ta + 512),
                                                start=(m == 0), stop=(m == KC - 1),
                                                skip_group_check=True))
                        nt_st = C.ct.inc(T and T.matmul(
                            ps2[:], onesc_bf[:], xsq_big[:, 512 * mq:512 * (mq + 1)],
                            start=(m == 0), stop=(m == KC - 1),
                            skip_group_check=True))
                    sq_used = nt_st
                _wt(V, C.ct, nt_st)
                V and V.tensor_copy(seg1[0:1, 512 * tt:512 * (tt + 1)], ps1[:])
                nv_seg_prev = C.cv.inc(V and V.tensor_copy(
                    seg2[0:1, 512 * tt:512 * (tt + 1)], ps2[:]))
            V and V.tensor_scalar(segt[:], seg1[:], 1.0 / D, None, ALU.mult)
            vpads(V)
            V and V.scalar_tensor_tensor(segu[:], segt[:], -float(D), segt[:],
                                         ALU.mult, ALU.mult)
            vpads(V)
            C.cv.inc(V and V.tensor_tensor(seg2[:], seg2[:], segu[:], ALU.add))
            nv_var = vpads(V)
            _wt(S, C.cv, nv_var)
            S and S.activation(seg1[:], seg2[:], AF.Ln, bias=epsln_sb[:], scale=1.0 / D)
            ns_rstd = C.cs.inc(S and S.activation(seg1[:], seg1[:], AF.Exp, scale=-0.5))
            _wt(V, C.cs, ns_rstd)
            V and V.tensor_copy(arow[0:1, t0:t0 + GT], seg1[:])
            V and V.scalar_tensor_tensor(segu[:], segt[:], -1.0, seg1[:], ALU.mult,
                                         ALU.mult)
            vpads(V)
            C.cv.inc(V and V.tensor_copy(brow[0:1, t0:t0 + GT], segu[:]))
            nv_rows = vpads(V)
            _wt(T, C.cv, nv_rows)
            for tt in range(2):
                ta = t0 + 512 * tt
                pa, pb = pbc[0], pbc[1]
                _wt(T, C.cv, getattr(phaseA, "_dummy", 0))
                C.ct.inc(T and T.matmul(pa[:], ones1r_bf[:],
                                        arow[0:1, ta:ta + 512], start=True, stop=True,
                                        skip_group_check=True))
                nt_bc = C.ct.inc(T and T.matmul(pb[:], ones1r_bf[:],
                                               brow[0:1, ta:ta + 512], start=True,
                                               stop=True, skip_group_check=True))
                _wt(V, C.ct, nt_bc)
                nv_h = 0
                for m in range(KC):
                    V and V.tensor_tensor(tmp_a[:], STRM(m, ta, ta + 512), pa[:],
                                          ALU.mult)
                    V and V.tensor_tensor(tmp_a[:], tmp_a[:], pb[:], ALU.add)
                    nv_h = C.cv.inc(V and V.tensor_scalar(
                        STRM(m, ta, ta + 512), tmp_a[:], cols[:, 32 + m:33 + m],
                        cols[:, 40 + m:41 + m], ALU.mult, ALU.add))
                _wt(T, C.cv, nv_h)
        if _DEBUG_TAPS:
            _wt(Y, C.cv, C.cv.val)
            C.qs.inc(Y and Y.dma_start(out=taps["tap_h"], in_=stream[:]))

    with nc.Block() as blk:
        run_walks(blk, phaseA)
    a_.close()

    nA = {k: c.val for k, c in CTRS.items()}

    b_ = ExitStack()
    bsb = lambda n, s, d: b_.enter_context(nc.sbuf_tensor(n, s, d))
    bpsf = lambda n, s: b_.enter_context(nc.psum_tensor(n, s, F32))
    q_raw = [bsb("q_raw%d" % i, [128, GT], BF16) for i in range(2)]
    k_raw = [bsb("k_raw%d" % i, [128, GT], BF16) for i in range(2)]
    qm0 = [bsb("qm0_%d" % i, [128, GT], BF16) for i in range(2)]
    qm1 = [bsb("qm1_%d" % i, [128, GT], BF16) for i in range(2)]
    km0 = [bsb("km0_%d" % i, [128, GT], BF16) for i in range(2)]
    km1 = [bsb("km1_%d" % i, [128, GT], BF16) for i in range(2)]
    ktok = [bsb("ktok%d" % i, [128, 2 * GT], BF16) for i in range(2)]
    v_g = bsb("v_g", [128, 8 * D], BF16)
    on_g = bsb("on_g", [128, H * GT], BF16)
    o4 = bsb("o4", [128, 512], BF16)
    o2sb = bsb("o2sb", [128, 512], BF16)
    scsb = [bsb("scsb%d" % i, [128, 128], BF16) for i in range(2)]
    Ssb = bsb("Ssb", [128, 256], BF16)
    Spark = bsb("Spark", [128, 256], BF16)
    Spark2 = [bsb("Spark2_%d" % i, [128, 256], BF16) for i in range(2)]
    wslab = [bsb("wslab%d" % i, [128, KC * 128], BF16) for i in range(2)]
    wvlab = bsb("wvlab", [128, KC * 512], BF16)
    wolab = [bsb("wolab%d" % i, [128, H * 128], BF16) for i in range(2)]
    xres = [bsb("xres%d" % i, [128, 512], F32) for i in range(2)]
    rstd_row = bsb("rstd_row", [1, 512], F32)
    rstd_bf = bsb("rstd_bf", [1, 512], BF16)
    h2stg = [bsb("h2stg%d" % i, [128, 512], BF16) for i in range(2)]
    tmpb = bsb("tmpb", [128, 512], BF16)
    xsqB = [bsb("xsqB%d" % i, [128, 512], BF16) for i in range(2)]

    pS = [bpsf("pS%d" % i, [128, 128]) for i in range(2)]
    po = bpsf("po", [128, 128])[:]
    pscb = bpsf("pscb", [128, 256])
    psc = [pscb[:, 0:128], pscb[:, 128:256]]
    psq = bpsf("psq", [1, 512])
    pbcB = bpsf("pbcB", [128, 512])
    pe_ = [bpsf("pe%d" % i, [128, 512]) for i in range(2)]

    def phaseB(me, E):
        Y, S, V, T, G = engs(me, E)
        for X in (Y, S, V, T, G):
            if X:
                for k, v in nA.items():
                    X.wait_ge(CTRS[k].sem, v)

        st = NS()
        st.pe_free = [0, 0]
        st.pe_free_cs = [0, 0]
        st.po_free_v = 0
        st.po_free_s = 0
        st.psc_free = [0, 0]
        st.nt_sc = [0, 0]
        st.psq_free = 0
        st.pbc_free = 0
        st.pending = None
        st.nv_rb = 0
        st.scsb_used2 = [0, 0]
        st.Ssb_mark = 0
        st.S_last_mm = 0
        st.restore_mark = 0
        st.wslab_used = [0, 0]
        st.wvlab_used = 0
        st.wolab_used = [0, 0]
        st.ong_used = 0
        st.park_dma = 0
        st.spark_dma = [0, 0]
        st.sp2_used = [0, 0]
        st.xres_used = [0, 0]
        st.m1slot = 0
        st.h2_dma = [0, 0]
        st.h2_red = [0, 0]
        st.xsqB_used = [0, 0]
        st.ktok_done = [0, 0]

        def use_pe(tt):
            _wt(T, C.cv, st.pe_free[tt])
            _wt(T, C.cs, st.pe_free_cs[tt])

        def make_head_pieces(g, h, hp):
            t0 = g * GT
            pieces = []

            def mk_wdma(src):
                def p():
                    _wt(S, C.ct, st.wslab_used[hp])
                    nw = C.qa.inc(S and S.dma_start(out=wslab[hp][:], in_=src))
                    st.wslab_dma = nw
                return p

            def mk_proj(which, tt):
                def p():
                    raw = (q_raw if which == "q" else k_raw)[hp]
                    pe = pe_[tt]
                    use_pe(tt)
                    _wt(T, C.qa, st.wslab_dma)
                    nt_p = 0
                    for k in range(KC):
                        nt_p = C.ct.inc(T and T.matmul(
                            pe[:], wslab[hp][:, 128 * k:128 * (k + 1)],
                            STRM(k, t0 + 512 * tt, t0 + 512 * (tt + 1)),
                            start=(k == 0), stop=(k == KC - 1),
                            skip_group_check=True))
                    if tt == 1:
                        st.wslab_used[hp] = nt_p
                    _wt(V, C.ct, nt_p)
                    nv_r = C.cv.inc(V and V.tensor_copy(
                        raw[:, 512 * tt:512 * (tt + 1)], pe[:]))
                    st.pe_free[tt] = nv_r
                    st.pe_free_cs[tt] = 0
                return p

            def mk_map(which, tt):
                def p():
                    raw = (q_raw if which == "q" else k_raw)[hp]
                    m0 = (qm0 if which == "q" else km0)[hp]
                    m1 = (qm1 if which == "q" else km1)[hp]
                    fmT = fmqT_sb if which == "q" else fmkT_sb
                    bp = cols[:, 28:29] if which == "q" else cols[:, 30:31]
                    bm = cols[:, 29:30] if which == "q" else cols[:, 31:32]
                    pe = pe_[tt]
                    use_pe(tt)
                    nt_y = C.ct.inc(T and T.matmul(pe[:], fmT[:],
                                                   raw[:, 512 * tt:512 * (tt + 1)],
                                                   start=True, stop=True,
                                                   skip_group_check=True))
                    _wt(S, C.ct, nt_y)
                    S and S.activation(m0[:, 512 * tt:512 * (tt + 1)], pe[:], AF.Exp,
                                       bias=bp, scale=1.0)
                    ns_e = C.cs.inc(S and S.activation(
                        m1[:, 512 * tt:512 * (tt + 1)], pe[:], AF.Exp, bias=bm,
                        scale=-1.0))
                    st.pe_free[tt] = 0
                    st.pe_free_cs[tt] = ns_e
                return p

            def mk_ktok(tt):
                def p():
                    pe = pe_[tt]
                    use_pe(tt)
                    nt_k = 0
                    for sl in range(4):
                        cc = tt * 4 + sl
                        C.ct.inc(T and T.matmul(pe[:, 128 * sl:128 * (sl + 1)],
                                                ones1r_bf[:], fmkbr_bf[:],
                                                start=True, stop=False,
                                                skip_group_check=True))
                        nt_k = C.ct.inc(T and T.matmul(
                            pe[:, 128 * sl:128 * (sl + 1)],
                            k_raw[hp][:, 128 * cc:128 * (cc + 1)], fmkT_sb[:],
                            start=False, stop=True, skip_group_check=True))
                    _wt(S, C.ct, nt_k)
                    dstp = ktok[hp][:].rearrange("p (c e) -> p c e", e=256)
                    srcp = pe[:].rearrange("p (c e) -> p c e", e=128)
                    S and S.activation(dstp[:, 4 * tt:4 * (tt + 1), 0:128], srcp,
                                       AF.Exp, scale=1.0)
                    ns_k = C.cs.inc(S and S.activation(
                        dstp[:, 4 * tt:4 * (tt + 1), 128:256], srcp, AF.Exp,
                        scale=-1.0))
                    st.pe_free[tt] = 0
                    st.pe_free_cs[tt] = ns_k
                    st.ktok_done[hp] = ns_k
                return p

            pieces.append(mk_wdma(sc_q[h]))
            pieces.append(mk_proj("q", 0))
            pieces.append(mk_proj("q", 1))
            pieces.append(mk_map("q", 0))
            pieces.append(mk_map("q", 1))
            pieces.append(mk_wdma(sc_k[h]))
            pieces.append(mk_proj("k", 0))
            pieces.append(mk_proj("k", 1))
            pieces.append(mk_map("k", 0))
            pieces.append(mk_map("k", 1))
            pieces.append(mk_ktok(0))
            pieces.append(mk_ktok(1))
            return pieces

        def emit_rms_apply():
            h_, c0 = st.pending
            _wt(T, C.cv, st.nv_rb)
            _wt(T, C.cv, st.pbc_free)
            nt_bc = C.ct.inc(T and T.matmul(pbcB[:], rmsr_bf[:], rstd_bf[:],
                                            start=True, stop=True,
                                            skip_group_check=True))
            _wt(V, C.ct, nt_bc)
            _wt(V, C.ct, st.ong_used)
            st.pbc_free = C.cv.inc(V and V.tensor_tensor(
                on_g[:, GT * h_ + c0:GT * h_ + c0 + 512], o4[:], pbcB[:], ALU.mult))
            st.pending = None

        def emit_scores(ci, hp, sl):
            if ci == 0:
                _wt(T, C.cs, st.ktok_done[hp])
            _wt(T, C.cv, st.psc_free[sl])
            C.ct.inc(T and T.matmul(psc[sl], km0[hp][:, 128 * ci:128 * (ci + 1)],
                                    qm0[hp][:, 128 * ci:128 * (ci + 1)],
                                    start=True, stop=False, skip_group_check=True))
            st.nt_sc[sl] = C.ct.inc(T and T.matmul(
                psc[sl], km1[hp][:, 128 * ci:128 * (ci + 1)],
                qm1[hp][:, 128 * ci:128 * (ci + 1)],
                start=False, stop=True, skip_group_check=True))

        def scan_chunk(g, h, ci, hp):
            first = (ci == 0 and g == 0)
            sl = ci % 2
            if ci == 0:
                emit_scores(0, hp, 0)
            if ci + 1 < GC:
                emit_scores(ci + 1, hp, (ci + 1) % 2)
            _wt(V, C.ct, st.nt_sc[sl])
            _wt(V, C.ct, st.scsb_used2[sl])
            nv_sc = C.cv.inc(V and V.tensor_tensor(scsb[sl][:], psc[sl], maskT[:],
                                                   ALU.mult))
            st.psc_free[sl] = nv_sc
            vsl = v_g[:, 1024 * ci + 128 * h:1024 * ci + 128 * (h + 1)]
            _wt(T, C.cv, nv_sc)
            _wt(T, C.cv, st.po_free_v)
            _wt(T, C.cs, st.po_free_s)
            nt_o = C.ct.inc(T and T.matmul(po, vsl, scsb[sl][:],
                                           start=True, stop=first,
                                           skip_group_check=True))
            st.scsb_used2[sl] = nt_o
            if not first:
                _wt(T, C.cv, st.Ssb_mark)
                C.ct.inc(T and T.matmul(po, Ssb[:, 0:128],
                                        qm0[hp][:, 128 * ci:128 * (ci + 1)],
                                        start=False, stop=False,
                                        skip_group_check=True))
                nt_o = C.ct.inc(T and T.matmul(po, Ssb[:, 128:256],
                                               qm1[hp][:, 128 * ci:128 * (ci + 1)],
                                               start=False, stop=True,
                                               skip_group_check=True))
            ksl = ktok[hp][:, 256 * ci:256 * (ci + 1)]
            s_start = (ci == 0 and g == 0)
            C.ct.inc(T and T.matmul(pS[0][:], ksl[:, 0:128], vsl,
                                    start=s_start, stop=False, skip_group_check=True))
            nt_S = C.ct.inc(T and T.matmul(pS[1][:], ksl[:, 128:256], vsl,
                                           start=s_start, stop=False,
                                           skip_group_check=True))
            st.S_last_mm = nt_S
            _wt(V, C.ct, nt_S)
            V and V.tensor_copy(Ssb[:, 0:128], pS[0][:])
            C.cv.inc(V and V.tensor_copy(Ssb[:, 128:256], pS[1][:]))
            st.Ssb_mark = vpads(V, 1)
            if st.pending is not None:
                emit_rms_apply()
            _wt(S, C.ct, nt_o)
            st.po_free_s = C.cs.inc(S and S.activation(
                o2sb[:, 128 * (ci % 4):128 * (ci % 4 + 1)], po, AF.Square,
                scale=2.0 ** -32))
            _wt(V, C.ct, nt_o)
            st.po_free_v = C.cv.inc(V and V.tensor_copy(
                o4[:, 128 * (ci % 4):128 * (ci % 4 + 1)], po))
            if ci % 4 == 3:
                _wt(T, C.cs, st.po_free_s)
                _wt(T, C.cs, st.psq_free)
                nt_sq = 0
                for cj in range(4):
                    nt_sq = C.ct.inc(T and T.matmul(
                        psq[0:1, 128 * cj:128 * (cj + 1)], onesc_bf[:],
                        o2sb[:, 128 * cj:128 * (cj + 1)], start=True, stop=True,
                        skip_group_check=True))
                _wt(S, C.ct, nt_sq)
                S and S.activation(rstd_row[:], psq[0:1, :], AF.Ln,
                                   bias=eps64_sb[:], scale=1.0 / 128.0)
                ns_r = C.cs.inc(S and S.activation(
                    rstd_row[:], rstd_row[:], AF.Exp, bias=nl2_sb[:], scale=-0.5))
                st.psq_free = ns_r
                _wt(V, C.cs, ns_r)
                C.cv.inc(V and V.tensor_copy(rstd_bf[:], rstd_row[:]))
                st.nv_rb = vpads(V, 1)
                st.pending = (h, (ci - 3) * 128)

        for g in range(NG):
            t0 = g * GT
            for half in range(2):
                _wt(S, C.ct, st.wvlab_used)
                nwv = C.qa.inc(S and S.dma_start(out=wvlab[:], in_=sc_v[half]))
                _wt(T, C.qa, nwv)
                for ts in range(8):
                    tt = ts % 2
                    pe = pe_[tt]
                    use_pe(tt)
                    nt_v = 0
                    for k in range(KC):
                        nt_v = C.ct.inc(T and T.matmul(
                            pe[:], STRM(k, t0 + 128 * ts, t0 + 128 * (ts + 1)),
                            wvlab[:, 512 * k:512 * (k + 1)],
                            start=(k == 0), stop=(k == KC - 1),
                            skip_group_check=True))
                    _wt(V, C.ct, nt_v)
                    nv_v = C.cv.inc(V and V.tensor_copy(
                        v_g[:, 1024 * ts + 512 * half:1024 * ts + 512 * (half + 1)],
                        pe[:]))
                    st.pe_free[tt] = nv_v
                    st.pe_free_cs[tt] = 0
                st.wvlab_used = C.ct.val
            for h in range(H):
                hidx = g * H + h
                hp = hidx % 2
                if hidx == 0:
                    for p in make_head_pieces(0, 0, 0):
                        p()
                if g > 0:
                    sp2 = Spark2[hidx % 2]
                    _wt(T, C.qs, st.spark_dma[hidx % 2])
                    _wt(T, C.cv, st.Ssb_mark)
                    C.ct.inc(T and T.matmul(pS[0][:], ident_bf[:], sp2[:, 0:128],
                                            start=True, stop=False,
                                            skip_group_check=True))
                    st.restore_mark = C.ct.inc(T and T.matmul(
                        pS[1][:], ident_bf[:], sp2[:, 128:256],
                        start=True, stop=False, skip_group_check=True))
                    st.sp2_used[hidx % 2] = st.restore_mark
                    _wt(V, C.ct, st.restore_mark)
                    V and V.tensor_copy(Ssb[:, 0:128], pS[0][:])
                    C.cv.inc(V and V.tensor_copy(Ssb[:, 128:256], pS[1][:]))
                    st.Ssb_mark = vpads(V, 1)
                if hidx + 1 < NG * H and hidx + 1 >= H:
                    slot = (hidx + 1) % 2
                    _wt(Y, C.ct, st.sp2_used[slot])
                    st.spark_dma[slot] = C.qs.inc(Y and Y.dma_start(
                        out=Spark2[slot][:], in_=sp_S[(hidx + 1) % H]))
                if hidx + 1 < NG * H:
                    gn, hn = divmod(hidx + 1, H)
                    nxt_pieces = make_head_pieces(gn, hn, (hidx + 1) % 2)
                else:
                    nxt_pieces = []
                while nxt_pieces:
                    nxt_pieces.pop(0)()
                for ci in range(GC):
                    scan_chunk(g, h, ci, hp)
                if st.pending is not None:
                    emit_rms_apply()
                if g < NG - 1:
                    _wt(V, C.ct, st.S_last_mm)
                    _wt(V, C.qs, st.park_dma)
                    V and V.tensor_copy(Spark[:, 0:128], pS[0][:])
                    nv_pk = C.cv.inc(V and V.tensor_copy(Spark[:, 128:256], pS[1][:]))
                    _wt(Y, C.cv, nv_pk)
                    st.park_dma = C.qs.inc(Y and Y.dma_start(out=sp_S[h], in_=Spark[:]))
                    st.Ssb_mark = nv_pk
            if _DEBUG_TAPS and g == 0:
                _wt(Y, C.cv, st.pbc_free)
                _wt(Y, C.cs, max(st.ktok_done))
                C.qs.inc(Y and Y.dma_start(out=taps["tap_ong"], in_=on_g[:]))
                C.qs.inc(Y and Y.dma_start(out=taps["tap_vg"], in_=v_g[:]))
                C.qs.inc(Y and Y.dma_start(
                    out=taps["tap_qm"].rearrange("p (a t) -> p a t", a=2)[:, 0],
                    in_=qm0[0][:]))
                C.qs.inc(Y and Y.dma_start(
                    out=taps["tap_qm"].rearrange("p (a t) -> p a t", a=2)[:, 1],
                    in_=qm1[0][:]))
                C.qs.inc(Y and Y.dma_start(
                    out=taps["tap_km"].rearrange("p (a t) -> p a t", a=2)[:, 0],
                    in_=km0[0][:]))
                C.qs.inc(Y and Y.dma_start(
                    out=taps["tap_km"].rearrange("p (a t) -> p a t", a=2)[:, 1],
                    in_=km1[0][:]))
                C.qs.inc(Y and Y.dma_start(out=taps["tap_ktok"], in_=ktok[0][:]))
                C.qs.inc(Y and Y.dma_start(out=taps["tap_Ssb"], in_=Ssb[:]))
                C.qs.inc(Y and Y.dma_start(out=taps["tap_o4"], in_=o4[:]))
                C.qs.inc(Y and Y.dma_start(out=taps["tap_o2"], in_=o2sb[:]))
                C.qs.inc(Y and Y.dma_start(out=taps["tap_rstd"], in_=rstd_row[:]))
            ong_new = 0
            _wt(T, C.cv, st.pbc_free)
            for m in range(KC):
                _wt(S, C.ct, st.wolab_used[m % 2])
                nwo = C.qa.inc(S and S.dma_start(out=wolab[m % 2][:], in_=sc_o[m]))
                _wt(T, C.qa, nwo)
                for t2 in range(2):
                    pe = pe_[t2]
                    use_pe(t2)
                    _wt(Y, C.cv, st.xres_used[t2])
                    nxr = C.qs.inc(Y and Y.dma_start(
                        out=xres[t2][:],
                        in_=xT[128 * m:128 * (m + 1),
                               t0 + 512 * t2:t0 + 512 * (t2 + 1)]))
                    nt_op = 0
                    for hh in range(H):
                        nt_op = C.ct.inc(T and T.matmul(
                            pe[:], wolab[m % 2][:, 128 * hh:128 * (hh + 1)],
                            on_g[:, GT * hh + 512 * t2:GT * hh + 512 * (t2 + 1)],
                            start=(hh == 0), stop=(hh == H - 1),
                            skip_group_check=True))
                    ong_new = nt_op
                    _wt(V, C.ct, nt_op)
                    _wt(V, C.qs, nxr)
                    nv_x2 = C.cv.inc(V and V.scalar_tensor_tensor(
                        STRM(m, t0 + 512 * t2, t0 + 512 * (t2 + 1)),
                        pe[:], cols[:, m:m + 1], xres[t2][:], ALU.mult, ALU.add))
                    st.xres_used[t2] = nv_x2
                    st.pe_free[t2] = nv_x2
                    st.pe_free_cs[t2] = 0
                st.wolab_used[m % 2] = C.ct.val
            st.ong_used = ong_new
            nv_x2g = C.cv.val

            nv_seg_prev = 0
            for tt in range(2):
                ta = t0 + 512 * tt
                nt_st = 0
                for m in range(KC):
                    _wt(S, C.cv, nv_x2g)
                    _wt(S, C.ct, st.xsqB_used[m % 2])
                    ns_sq = C.cs.inc(S and S.activation(xsqB[m % 2][:],
                                                        STRM(m, ta, ta + 512),
                                                        AF.Square))
                    _wt(T, C.cs, ns_sq)
                    if m == 0:
                        _wt(T, C.cv, nv_x2g)
                        _wt(T, C.cv, nv_seg_prev)
                        _wt(T, C.cs, st.psq_free)
                        _wt(T, C.cv, st.pbc_free)
                    C.ct.inc(T and T.matmul(psq[0:1, :], onesc_bf[:], STRM(m, ta, ta + 512),
                                            start=(m == 0), stop=(m == KC - 1),
                                            skip_group_check=True))
                    nt_st = C.ct.inc(T and T.matmul(pbcB[0:1, :], onesc_bf[:],
                                                    xsqB[m % 2][:],
                                                    start=(m == 0), stop=(m == KC - 1),
                                                    skip_group_check=True))
                    st.xsqB_used[m % 2] = nt_st
                _wt(V, C.ct, nt_st)
                V and V.tensor_copy(seg1[0:1, 512 * tt:512 * (tt + 1)], psq[0:1, :])
                nv_seg_prev = C.cv.inc(V and V.tensor_copy(
                    seg2[0:1, 512 * tt:512 * (tt + 1)], pbcB[0:1, :]))
            st.psq_free = 0
            st.pbc_free = nv_seg_prev
            V and V.tensor_scalar(segt[:], seg1[:], 1.0 / D, None, ALU.mult)
            vpads(V)
            V and V.scalar_tensor_tensor(segu[:], segt[:], -float(D), segt[:],
                                         ALU.mult, ALU.mult)
            vpads(V)
            C.cv.inc(V and V.tensor_tensor(seg2[:], seg2[:], segu[:], ALU.add))
            nv_var = vpads(V)
            _wt(S, C.cv, nv_var)
            S and S.activation(seg1[:], seg2[:], AF.Ln, bias=epsln_sb[:], scale=1.0 / D)
            ns_rstd = C.cs.inc(S and S.activation(seg1[:], seg1[:], AF.Exp, scale=-0.5))
            _wt(V, C.cs, ns_rstd)
            V and V.tensor_copy(arow[0:1, t0:t0 + GT], seg1[:])
            V and V.scalar_tensor_tensor(segu[:], segt[:], -1.0, seg1[:], ALU.mult,
                                         ALU.mult)
            vpads(V)
            C.cv.inc(V and V.tensor_copy(brow[0:1, t0:t0 + GT], segu[:]))
            nv_rows = vpads(V)
            _wt(T, C.cv, nv_rows)
            for m in range(KC):
                for tt in range(2):
                    ta = t0 + 512 * tt
                    bi = (m * 2 + tt) % 2
                    pe = pe_[bi]
                    use_pe(bi)
                    _wt(T, C.cv, st.pbc_free)
                    C.ct.inc(T and T.matmul(pbcB[:], gam_b[0:1, 128 * m:128 * (m + 1)],
                                            arow[0:1, ta:ta + 512], start=True,
                                            stop=True, skip_group_check=True))
                    C.ct.inc(T and T.matmul(pe[:], gam_b[0:1, 128 * m:128 * (m + 1)],
                                            brow[0:1, ta:ta + 512], start=True,
                                            stop=False, skip_group_check=True))
                    nt_bc = C.ct.inc(T and T.matmul(
                        pe[:], sft_b[0:1, 128 * m:128 * (m + 1)], onesr512_bf[:],
                        start=False, stop=True, skip_group_check=True))
                    _wt(V, C.ct, nt_bc)
                    _wt(V, C.qs, st.h2_dma[bi])
                    _wt(V, C.cg, st.h2_red[bi])
                    V and V.tensor_tensor(tmpb[:], STRM(m, ta, ta + 512), pbcB[:],
                                          ALU.mult)
                    nv_h2 = C.cv.inc(V and V.tensor_tensor(h2stg[bi][:], tmpb[:],
                                                           pe[:], ALU.add))
                    st.pe_free[bi] = nv_h2
                    st.pe_free_cs[bi] = 0
                    st.pbc_free = nv_h2
                    _wt(G, C.cv, nv_h2)
                    st.h2_red[bi] = C.cg.inc(G and G.tensor_reduce(
                        mprt[0:1, st.m1slot:st.m1slot + 1], h2stg[bi][:],
                        AX.XYZWC, ALU.abs_max))
                    st.m1slot += 1
                    _wt(Y, C.cv, nv_h2)
                    st.h2_dma[bi] = C.qs.inc(Y and Y.dma_start(
                        out=sp_h2[m][:, ta:ta + 512], in_=h2stg[bi][:]))
                    C.qs.inc(Y and Y.dma_start(out=sp_x2[m][:, ta:ta + 512],
                                               in_=STRM(m, ta, ta + 512)))
        if _DEBUG_TAPS:
            _wt(Y, C.cv, C.cv.val)
            C.qs.inc(Y and Y.dma_start(out=taps["tap_x2"], in_=stream[:]))

    with nc.Block() as blk:
        run_walks(blk, phaseB)
    b_.close()
    ab.close()

    nB = {k: c.val for k, c in CTRS.items()}

    c1 = ExitStack()
    csb = lambda n, s, d: c1.enter_context(nc.sbuf_tensor(n, s, d))
    cpsf = lambda n, s: c1.enter_context(nc.psum_tensor(n, s, F32))
    xhat = csb("xhat", [128, KC * NT], BF16)
    xq32 = [csb("xq32_%d" % i, [128, 512], I32) for i in range(2)]
    wblk = [csb("wblk%d" % i, [128, KC * 128], BF16) for i in range(2)]
    fstg = [csb("fstg%d" % i, [128, NT], BF16) for i in range(2)]
    gtile = csb("gtile", [128, NT], BF16)
    uptile = csb("uptile", [128, NT], BF16)
    ptile = csb("ptile", [128, NT], BF16)
    optile = csb("optile", [128, NT], BF16)
    vredc = csb("vredc", [128, 1], F32)
    mcol4 = csb("mcol4", [128, 1], F32)
    mcol5 = csb("mcol5", [128, 1], F32)
    sigf = csb("sigf", [128, NT], F32)
    qi32 = csb("qi32", [128, NT], I32)
    sq32 = csb("sq32", [128, NT], I32)
    g232 = csb("g232", [128, NT], I32)
    u232 = csb("u232", [128, NT], I32)
    pf = [cpsf("pf%d" % i, [128, 512]) for i in range(4)]
    pcol = cpsf("pcol", [128, 16])
    prowC = c1.enter_context(nc.psum_tensor("prowC", [1, 128], BF16))

    def phaseC1(me, E):
        Y, S, V, T, G = engs(me, E)
        for X in (Y, S, V, T, G):
            if X:
                for k, v in nB.items():
                    X.wait_ge(CTRS[k].sem, v)

        if _DEBUG_TAPS:
            C.qs.inc(Y and Y.dma_start(out=taps["tap_mprt"], in_=mprt[:]))
        ng1 = finalize_and_allreduce(me, E, [((0, 64), 0)], "1")
        _wt(V, C.qg, ng1)
        nv_s1 = make_scale_cols(me, E, 0, 16, 17, pcol, 0)

        nv_x = 0
        for m in range(KC):
            nrd = C.qs.inc(Y and Y.dma_start(out=xhat[:, NT * m:NT * (m + 1)],
                                             in_=sp_h2[m]))
            for tt in range(8):
                sl = xhat[:, NT * m + 512 * tt: NT * m + 512 * (tt + 1)]
                _wt(V, C.qs, nrd)
                _wt(V, C.cv, nv_s1)
                V and V.tensor_scalar(xq32[tt % 2][:], sl, cols[:, 16:17], None,
                                      ALU.mult)
                nv_x = C.cv.inc(V and V.tensor_scalar(sl, xq32[tt % 2][:],
                                                      cols[:, 17:18], None, ALU.mult))
        if _DEBUG_TAPS:
            _wt(Y, C.cv, nv_x)
            C.qs.inc(Y and Y.dma_start(out=taps["tap_xhat"], in_=xhat[:]))

        stG = NS()
        stG.pf_free = [0, 0, 0, 0]
        stG.pf_free_cs = [0, 0, 0, 0]
        stG.stg_dma = [0, 0]
        stG.stg_red = [0, 0]
        stG.wblk_used = [0, 0]
        stG.last_e = ("v", 0)
        stG.nw = [0, 0]
        stG.nw[0] = C.qa.inc(S and S.dma_start(out=wblk[0][:], in_=sc_gt[0]))
        for i in range(IC):
            si = i % 2
            if i + 1 < IC:
                s2 = (i + 1) % 2
                _wt(S, C.ct, stG.wblk_used[s2])
                stG.nw[s2] = C.qa.inc(S and S.dma_start(out=wblk[s2][:],
                                                        in_=sc_gt[i + 1]))
            _wt(T, C.qa, stG.nw[si])
            _wt(T, C.cv, nv_x)
            nv_e = 0
            for t in range(8):
                b = t % 4
                _wt(T, C.cv, stG.pf_free[b])
                _wt(T, C.cs, stG.pf_free_cs[b])
                nt_f = 0
                for k in range(KC):
                    nt_f = C.ct.inc(T and T.matmul(
                        pf[b][:], wblk[si][:, 128 * k:128 * (k + 1)],
                        xhat[:, NT * k + 512 * t:NT * k + 512 * (t + 1)],
                        start=(k == 0), stop=(k == KC - 1), skip_group_check=True))
                if t == 0:
                    _wt(V, C.qs, stG.stg_dma[si])
                    _wt(S, C.qs, stG.stg_dma[si])
                    _wt(V, C.cg, stG.stg_red[si])
                    _wt(S, C.cg, stG.stg_red[si])
                if t % 2 == 0:
                    _wt(V, C.ct, nt_f)
                    nv_e = C.cv.inc(V and V.tensor_copy(
                        fstg[si][:, 512 * t:512 * (t + 1)], pf[b][:]))
                    stG.pf_free[b] = nv_e
                    stG.pf_free_cs[b] = 0
                    stG.last_e = ("v", nv_e)
                else:
                    _wt(S, C.ct, nt_f)
                    ns_e = C.cs.inc(S and S.copy(fstg[si][:, 512 * t:512 * (t + 1)],
                                                 pf[b][:]))
                    stG.pf_free[b] = 0
                    stG.pf_free_cs[b] = ns_e
                    stG.last_e = ("s", ns_e)
            stG.wblk_used[si] = C.ct.val
            _wt(Y, C.cv, nv_e)
            _wt(Y, C.cs, stG.last_e[1] if stG.last_e[0] == "s" else 0)
            stG.stg_dma[si] = C.qs.inc(Y and Y.dma_start(out=sp_gate[i],
                                                         in_=fstg[si][:]))
            _wt(V, C.cs, stG.last_e[1] if stG.last_e[0] == "s" else 0)
            _wt(G, C.cv, nv_e)
            _wt(G, C.cs, stG.last_e[1] if stG.last_e[0] == "s" else 0)
            stG.stg_red[si] = C.cg.inc(G and G.tensor_reduce(
                mprt[0:1, i:i + 1], fstg[si][:], AX.XYZWC, ALU.abs_max))

        ng3 = finalize_and_allreduce(me, E, [((0, 24), 1)], "3",
                                     wait_cg=max(stG.stg_red))
        _wt(V, C.qg, ng3)
        make_scale_cols(me, E, 1, 18, 19, pcol, 0, pcol_free=nv_s1)
        C.cv.inc(V and V.tensor_scalar(cols[:, 20:21], pcol[:, 1:2],
                                       2.0 ** -16, None, ALU.mult))
        nv_s3e = vpads(V)

        stU = NS()
        stU.pf_free = list(stG.pf_free)
        stU.pf_free_cs = list(stG.pf_free_cs)
        stU.stg_dma = list(stG.stg_dma)
        stU.stg_red = list(stG.stg_red)
        stU.wblk_used = list(stG.wblk_used)
        stU.last_e = stG.last_e
        f2 = NS()
        V and V.memset(mcol4[:], 0.0)
        V and V.memset(mcol5[:], 0.0)
        f2.p_dma = 0
        f2.q_done = 0
        f2.sq_done = 0
        f2.p_done = 0
        f2.red = 0
        stU.nw = [0, 0]
        _wt(S, C.ct, stU.wblk_used[0])
        stU.nw[0] = C.qa.inc(S and S.dma_start(out=wblk[0][:], in_=sc_up[0]))
        for i in range(IC):
            si = i % 2
            if i + 1 < IC:
                s2 = (i + 1) % 2
                _wt(S, C.ct, stU.wblk_used[s2])
                stU.nw[s2] = C.qa.inc(S and S.dma_start(out=wblk[s2][:],
                                                        in_=sc_up[i + 1]))
            _wt(S, C.cv, f2.p_done)
            _wt(S, C.cv, f2.sq_done)
            ng_t = C.qa.inc(S and S.dma_start(out=gtile[:], in_=sp_gate[i]))
            _wt(T, C.qa, stU.nw[si])
            nv_e = 0
            for t in range(8):
                b = t % 4
                _wt(T, C.cv, stU.pf_free[b])
                _wt(T, C.cs, stU.pf_free_cs[b])
                nt_f = 0
                for k in range(KC):
                    nt_f = C.ct.inc(T and T.matmul(
                        pf[b][:], wblk[si][:, 128 * k:128 * (k + 1)],
                        xhat[:, NT * k + 512 * t:NT * k + 512 * (t + 1)],
                        start=(k == 0), stop=(k == KC - 1), skip_group_check=True))
                if t == 0:
                    _wt(V, C.qs, stU.stg_dma[si])
                    _wt(S, C.qs, stU.stg_dma[si])
                    _wt(V, C.cg, stU.stg_red[si])
                    _wt(S, C.cg, stU.stg_red[si])
                if t % 2 == 0:
                    _wt(V, C.ct, nt_f)
                    nv_e = C.cv.inc(V and V.tensor_copy(
                        fstg[si][:, 512 * t:512 * (t + 1)], pf[b][:]))
                    stU.pf_free[b] = nv_e
                    stU.pf_free_cs[b] = 0
                    stU.last_e = ("v", nv_e)
                else:
                    _wt(S, C.ct, nt_f)
                    ns_e = C.cs.inc(S and S.copy(fstg[si][:, 512 * t:512 * (t + 1)],
                                                 pf[b][:]))
                    stU.pf_free[b] = 0
                    stU.pf_free_cs[b] = ns_e
                    stU.last_e = ("s", ns_e)
            stU.wblk_used[si] = C.ct.val
            _wt(Y, C.cv, nv_e)
            _wt(Y, C.cs, stU.last_e[1] if stU.last_e[0] == "s" else 0)
            stU.stg_dma[si] = C.qs.inc(Y and Y.dma_start(out=sp_up[i], in_=fstg[si][:]))
            _wt(V, C.cs, stU.last_e[1] if stU.last_e[0] == "s" else 0)
            _wt(G, C.cv, nv_e)
            _wt(G, C.cs, stU.last_e[1] if stU.last_e[0] == "s" else 0)
            stU.stg_red[si] = C.cg.inc(G and G.tensor_reduce(
                mprt[0:1, 24 + i:25 + i], fstg[si][:], AX.XYZWC, ALU.abs_max))
            _wt(S, C.qa, ng_t)
            _wt(S, C.cv, nv_s3e)
            S and S.activation(qi32[:], gtile[:], AF.Copy, scale=cols[:, 18:19])
            f2.q_done = C.cs.inc(S and S.activation(sigf[:], qi32[:], AF.Sigmoid,
                                                    scale=cols[:, 19:20]))
            _wt(V, C.cs, f2.q_done)
            f2.sq_done = C.cv.inc(V and V.tensor_scalar(sq32[:], sigf[:], 65536.0,
                                                        65535.0, ALU.mult, ALU.min))
            _wt(V, C.qs, f2.p_dma)
            f2.p_done = C.cv.inc(V and V.scalar_tensor_tensor(
                ptile[:], qi32[:], cols[:, 20:21], sq32[:], ALU.mult, ALU.mult))
            _wt(Y, C.cv, f2.p_done)
            f2.p_dma = C.qs.inc(Y and Y.dma_start(out=sp_p[i], in_=ptile[:]))
            V and V.tensor_reduce(vredc[:, 0:1].unsqueeze(2),
                                  ptile[:].rearrange("p (a t) -> p a t", a=1),
                                  AX.X, ALU.max, apply_absolute_value=True)
            vpads(V, 1)
            f2.vm = C.cv.inc(V and V.tensor_tensor(mcol4[:], mcol4[:], vredc[:],
                                                   ALU.max))

        _wt(G, C.cv, f2.vm)
        ng_m4 = C.cg.inc(G and G.tensor_reduce(mprt[0:1, 48:49], mcol4[:],
                                               AX.XYZWC, ALU.abs_max))
        ng24 = finalize_and_allreduce(me, E, [((24, 48), 2), ((48, 49), 3)], "24",
                                      wait_cg=max(max(stU.stg_red), ng_m4))
        _wt(V, C.qg, ng24)
        make_scale_cols(me, E, 2, 21, 22, pcol, 0, pcol_free=nv_s3e)
        nv_mk4 = make_scale_cols(me, E, 3, 23, 24, pcol, 0, pcol_free=nv_s3e)
        C.cv.inc(V and V.tensor_tensor(cols[:, 25:26], pcol[:, 1:2],
                                       cols[:, 22:23], ALU.mult))
        nv_s42 = vpads(V)

        fb = NS()
        fb.o_dma = 0
        fb.g2 = 0
        fb.u2 = 0
        fb.op = 0
        fb.red = 0
        for i in range(IC):
            _wt(Y, C.cv, fb.g2)
            nrp = C.qs.inc(Y and Y.dma_start(out=ptile[:], in_=sp_p[i]))
            _wt(S, C.cv, fb.op)
            nru = C.qa.inc(S and S.dma_start(out=uptile[:], in_=sp_up[i]))
            _wt(V, C.qs, nrp)
            _wt(V, C.cv, nv_s42)
            fb.g2 = C.cv.inc(V and V.tensor_scalar(g232[:], ptile[:], cols[:, 23:24],
                                                   None, ALU.mult))
            _wt(S, C.qa, nru)
            _wt(S, C.cv, nv_s42)
            fb.u2 = C.cs.inc(S and S.activation(u232[:], uptile[:], AF.Copy,
                                                scale=cols[:, 21:22]))
            _wt(V, C.cs, fb.u2)
            _wt(V, C.qs, fb.o_dma)
            fb.op = C.cv.inc(V and V.scalar_tensor_tensor(
                optile[:], g232[:], cols[:, 25:26], u232[:], ALU.mult, ALU.mult))
            _wt(Y, C.cv, fb.op)
            fb.o_dma = C.qs.inc(Y and Y.dma_start(out=sp_o[i], in_=optile[:]))
            V and V.tensor_reduce(vredc[:, 0:1].unsqueeze(2),
                                  optile[:].rearrange("p (a t) -> p a t", a=1),
                                  AX.X, ALU.max, apply_absolute_value=True)
            vpads(V, 1)
            fb.vm = C.cv.inc(V and V.tensor_tensor(mcol5[:], mcol5[:], vredc[:],
                                                   ALU.max))

        _wt(G, C.cv, fb.vm)
        ng_m5 = C.cg.inc(G and G.tensor_reduce(mprt[0:1, 72:73], mcol5[:],
                                               AX.XYZWC, ALU.abs_max))
        ng5 = finalize_and_allreduce(me, E, [((72, 73), 4)], "5", wait_cg=ng_m5)
        _wt(V, C.qg, ng5)
        make_scale_cols(me, E, 4, 26, 27, pcol, 0, pcol_free=nv_s42)
        if _DEBUG_TAPS:
            _wt(Y, C.cv, C.cv.val)
            C.qs.inc(Y and Y.dma_start(out=taps["tap_scal"], in_=cols[:]))
            C.qs.inc(Y and Y.dma_start(out=taps["tap_mrow"], in_=mrow[:]))

    with nc.Block() as blk:
        run_walks(blk, phaseC1)
    c1.close()

    nC1 = {k: c.val for k, c in CTRS.items()}

    c2 = ExitStack()
    c2sb = lambda n, s, d: c2.enter_context(nc.sbuf_tensor(n, s, d))
    c2ps = lambda n, s: c2.enter_context(nc.psum_tensor(n, s, F32))
    wdnall = c2sb("wdnall", [128, KC * IC * 128], BF16)
    ohat = [c2sb("ohat%d" % i, [128, IC * 512], BF16) for i in range(2)]
    oq32 = [c2sb("oq32_%d" % i, [128, 512], I32) for i in range(2)]
    opin = [c2sb("opin%d" % i, [128, 512], BF16) for i in range(2)]
    x2in = [c2sb("x2in%d" % i, [128, 512], BF16) for i in range(2)]
    ostg = [c2sb("ostg%d" % i, [128, 512], F32) for i in range(2)]
    pd = [c2ps("pd%d" % i, [128, 512]) for i in range(4)]

    def phaseC2(me, E):
        Y, S, V, T, G = engs(me, E)
        for X in (Y, S, V, T, G):
            if X:
                for k, v in nC1.items():
                    X.wait_ge(CTRS[k].sem, v)
        nwd = 0
        for m in range(KC):
            nwd = C.qa.inc(S and S.dma_start(
                out=wdnall[:, IC * 128 * m:IC * 128 * (m + 1)], in_=sc_dn[m]))
        stD = NS()
        stD.oh_used = [0, 0]
        stD.pd_free = [0, 0, 0, 0]
        stD.out_dma = [0, 0]
        stD.opin_free = [0, 0]
        stD.x2_used = [0, 0]
        stD.nv_oh = [0, 0]
        nfinal = 0

        def stage(t2):
            ta2 = 512 * t2
            ti2 = t2 % 2
            nv_oh = 0
            for j in range(IC):
                jj = j % 2
                _wt(Y, C.cv, stD.opin_free[jj])
                nro = C.qs.inc(Y and Y.dma_start(out=opin[jj][:],
                                                 in_=sp_o[j][:, ta2:ta2 + 512]))
                _wt(V, C.qs, nro)
                if j < 2:
                    _wt(V, C.ct, stD.oh_used[ti2])
                V and V.tensor_scalar(oq32[jj][:], opin[jj][:], cols[:, 26:27],
                                      None, ALU.mult)
                nv_oh = C.cv.inc(V and V.tensor_scalar(
                    ohat[ti2][:, 512 * j:512 * (j + 1)], oq32[jj][:], cols[:, 27:28],
                    None, ALU.mult))
                stD.opin_free[jj] = nv_oh
            stD.nv_oh[ti2] = nv_oh

        stage(0)
        for t in range(8):
            ta = 512 * t
            ti = t % 2
            if t + 1 < 8:
                stage(t + 1)
            _wt(T, C.cv, stD.nv_oh[ti])
            _wt(T, C.qa, nwd)
            for m in range(KC):
                b = m % 4
                _wt(T, C.cv, stD.pd_free[b])
                nt_d = 0
                for j in range(IC):
                    nt_d = C.ct.inc(T and T.matmul(
                        pd[b][:],
                        wdnall[:, IC * 128 * m + 128 * j:IC * 128 * m + 128 * (j + 1)],
                        ohat[ti][:, 512 * j:512 * (j + 1)],
                        start=(j == 0), stop=(j == IC - 1), skip_group_check=True))
                stD.oh_used[ti] = nt_d
                mi = m % 2
                _wt(S, C.cv, stD.x2_used[mi])
                nx2 = C.qa.inc(S and S.dma_start(out=x2in[mi][:],
                                                 in_=sp_x2[m][:, ta:ta + 512]))
                _wt(V, C.ct, nt_d)
                _wt(V, C.qa, nx2)
                _wt(V, C.qa, stD.out_dma[mi])
                nv_o = C.cv.inc(V and V.scalar_tensor_tensor(
                    ostg[mi][:], pd[b][:], cols[:, 8 + m:9 + m], x2in[mi][:],
                    ALU.mult, ALU.add))
                stD.pd_free[b] = nv_o
                stD.x2_used[mi] = nv_o
                _wt(S, C.cv, nv_o)
                nfinal = C.qa.inc(S and S.dma_start(
                    out=out_fm[128 * m:128 * (m + 1), ta:ta + 512], in_=ostg[mi][:]))
                stD.out_dma[mi] = nfinal
        for X in (Y, S, V, T, G):
            X and X.wait_ge(C.qa.sem, nfinal)

    with nc.Block() as blk:
        run_walks(blk, phaseC2)
    c2.close()
    glob.close()

    return nc


_NC_CACHE = {}
_LAST_RES = None


def _get_nc():
    key = "taps" if _DEBUG_TAPS else "plain"
    if key not in _NC_CACHE:
        _NC_CACHE[key] = build_program()
    return _NC_CACHE[key]


def kernel(L0_attn_x, L0_attn_Wq, L0_attn_Wk, L0_attn_Wv, L0_attn_Wo,
           L0_attn_fmq_w, L0_attn_fmq_b, L0_attn_fmk_w, L0_attn_fmk_b,
           L0_attn_rms_w, L0_ffn_w_gate, L0_ffn_w_up, L0_ffn_w_down,
           L0_c, L0_wmod):
    x = np.asarray(L0_attn_x, np.float32)
    common = {
        "wqT": np.ascontiguousarray(np.asarray(L0_attn_Wq, np.float32).T),
        "wkT": np.ascontiguousarray(np.asarray(L0_attn_Wk, np.float32).T),
        "wvT": np.ascontiguousarray(np.asarray(L0_attn_Wv, np.float32).T),
        "woT": np.ascontiguousarray(np.asarray(L0_attn_Wo, np.float32).T),
        "fmqT": np.ascontiguousarray(np.asarray(L0_attn_fmq_w, np.float32).T),
        "fmkT": np.ascontiguousarray(np.asarray(L0_attn_fmk_w, np.float32).T),
        "fmq_b": np.ascontiguousarray(np.asarray(L0_attn_fmq_b, np.float32)[:, None]),
        "fmk_b": np.ascontiguousarray(np.asarray(L0_attn_fmk_b, np.float32)[:, None]),
        "fmk_b_row": np.ascontiguousarray(np.asarray(L0_attn_fmk_b, np.float32)[None, :]),
        "rms_row": np.ascontiguousarray(np.asarray(L0_attn_rms_w, np.float32)[None, :]),
        "wmod": np.ascontiguousarray(np.asarray(L0_wmod, np.float32)),
        "w_up": np.ascontiguousarray(np.asarray(L0_ffn_w_up, np.float32)),
        "w_gate": np.ascontiguousarray(np.asarray(L0_ffn_w_gate, np.float32)),
        "w_down": np.ascontiguousarray(np.asarray(L0_ffn_w_down, np.float32)),
    }
    cmat = np.asarray(L0_c, np.float32)
    in_maps = []
    for c in range(8):
        m = dict(common)
        m["xT"] = np.ascontiguousarray(x[c].T)
        m["c_row"] = np.ascontiguousarray(cmat[c:c + 1])
        in_maps.append(m)
    nc = _get_nc()
    res = run_bass_kernel_spmd(nc, in_maps, core_ids=list(range(8)))
    global _LAST_RES
    _LAST_RES = res
    out = np.stack([np.asarray(res.results[c]["out_fm"]).T for c in range(8)])
    return np.ascontiguousarray(out, dtype=np.float32)


# version 32
# speedup vs baseline: 1.0248x; 1.0248x over previous
import numpy as np
from contextlib import ExitStack

import concourse.bass as bass
import concourse.mybir as mybir
from concourse.bass_utils import run_bass_kernel_spmd

F32 = mybir.dt.float32
BF16 = mybir.dt.bfloat16
I32 = mybir.dt.int32
AF = mybir.ActivationFunctionType
ALU = mybir.AluOpType
AX = mybir.AxisListType

D = 1024
NT = 4096
H = 8
DK = 128
CH = 128
GC = 8
NG = 4
GT = GC * CH
KC = 8
IF = 3072
IC = 24
LN16 = float(np.log(16.0))
QM_ACT = 32767.0
QM_W = 7.0
EPS_LN = 1e-5
EPS_RMS = 1e-5

ENGS = ("sync", "scalar", "vector", "tensor", "gpsimd")

_DEBUG_TAPS = False


class Ctr:
    __slots__ = ("sem", "step", "n")

    def __init__(self, sem, step=1):
        self.sem = sem
        self.step = step
        self.n = 0

    def inc(self, inst):
        self.n += 1
        if inst:
            inst.then_inc(self.sem, self.step)
        return self.n * self.step

    @property
    def val(self):
        return self.n * self.step


class NS:
    pass


def _wt(X, ctr, val):
    if X and val > 0:
        X.wait_ge(ctr.sem, val)


def build_program():
    nc = bass.Bass("TRN2", target_bir_lowering=False, debug=False)

    xT = nc.dram_tensor("xT", [D, NT], F32, kind="ExternalInput").ap()
    c_row = nc.dram_tensor("c_row", [1, D], F32, kind="ExternalInput").ap()
    wqT = nc.dram_tensor("wqT", [D, D], F32, kind="ExternalInput").ap()
    wkT = nc.dram_tensor("wkT", [D, D], F32, kind="ExternalInput").ap()
    wvT = nc.dram_tensor("wvT", [D, D], F32, kind="ExternalInput").ap()
    woT = nc.dram_tensor("woT", [D, D], F32, kind="ExternalInput").ap()
    fmqT = nc.dram_tensor("fmqT", [DK, DK], F32, kind="ExternalInput").ap()
    fmkT = nc.dram_tensor("fmkT", [DK, DK], F32, kind="ExternalInput").ap()
    fmq_b = nc.dram_tensor("fmq_b", [DK, 1], F32, kind="ExternalInput").ap()
    fmk_b = nc.dram_tensor("fmk_b", [DK, 1], F32, kind="ExternalInput").ap()
    fmk_b_row = nc.dram_tensor("fmk_b_row", [1, DK], F32, kind="ExternalInput").ap()
    rms_row = nc.dram_tensor("rms_row", [1, DK], F32, kind="ExternalInput").ap()
    wmod = nc.dram_tensor("wmod", [D, 6 * D], F32, kind="ExternalInput").ap()
    w_up = nc.dram_tensor("w_up", [IF, D], F32, kind="ExternalInput").ap()
    w_gate = nc.dram_tensor("w_gate", [IF, D], F32, kind="ExternalInput").ap()
    w_down = nc.dram_tensor("w_down", [D, IF], F32, kind="ExternalInput").ap()
    out_fm = nc.dram_tensor("out_fm", [D, NT], F32, kind="ExternalOutput").ap()

    dtn = nc.dram_tensor
    sc_q = dtn("sc_q", [H, 128, KC * 128], BF16).ap()
    sc_k = dtn("sc_k", [H, 128, KC * 128], BF16).ap()
    sc_v = dtn("sc_v", [2, 128, KC * 512], BF16).ap()
    sc_o = dtn("sc_o", [KC, 128, H * 128], BF16).ap()
    sc_up = dtn("sc_up", [IC, 128, KC * 128], BF16).ap()
    sc_gt = dtn("sc_gt", [IC, 128, KC * 128], BF16).ap()
    sc_dn = dtn("sc_dn", [KC, 128, IC * 128], BF16).ap()
    sp_gate = dtn("sp_gate", [IC, 128, NT], BF16).ap()
    sp_up = dtn("sp_up", [IC, 128, NT], BF16).ap()
    sp_p = dtn("sp_p", [IC, 128, NT], BF16).ap()
    sp_o = dtn("sp_o", [IC, 128, NT], BF16).ap()
    sp_x2 = dtn("sp_x2", [KC, 128, NT], BF16).ap()
    sp_h2 = dtn("sp_h2", [KC, 128, NT], BF16).ap()
    sp_S = dtn("sp_S", [H, 128, 256], BF16).ap()
    ar_in = {k: dtn("ar%s_in" % k, [1, 2], F32).ap() for k in ("1", "3", "24", "5")}
    ar_out = {k: dtn("ar%s_out" % k, [1, 2], F32).ap() for k in ("1", "3", "24", "5")}

    taps = {}
    if _DEBUG_TAPS:
        taps["tap_mod"] = dtn("tap_mod", [1, 6 * D], BF16, kind="ExternalOutput").ap()
        taps["tap_h"] = dtn("tap_h", [128, KC * NT], BF16, kind="ExternalOutput").ap()
        taps["tap_x2"] = dtn("tap_x2", [128, KC * NT], BF16, kind="ExternalOutput").ap()
        taps["tap_wup"] = dtn("tap_wup", [128, KC * 128], BF16, kind="ExternalOutput").ap()
        taps["tap_xhat"] = dtn("tap_xhat", [128, KC * NT], BF16, kind="ExternalOutput").ap()
        taps["tap_scal"] = dtn("tap_scal", [128, 64], F32, kind="ExternalOutput").ap()
        taps["tap_ong"] = dtn("tap_ong", [128, H * GT], BF16, kind="ExternalOutput").ap()
        taps["tap_vg"] = dtn("tap_vg", [128, 8 * D], BF16, kind="ExternalOutput").ap()
        taps["tap_qm"] = dtn("tap_qm", [128, 2 * GT], BF16, kind="ExternalOutput").ap()
        taps["tap_km"] = dtn("tap_km", [128, 2 * GT], BF16, kind="ExternalOutput").ap()
        taps["tap_ktok"] = dtn("tap_ktok", [128, 2 * GT], BF16, kind="ExternalOutput").ap()
        taps["tap_Ssb"] = dtn("tap_Ssb", [128, 256], BF16, kind="ExternalOutput").ap()
        taps["tap_mprt"] = dtn("tap_mprt", [1, 96], F32, kind="ExternalOutput").ap()
        taps["tap_mrow"] = dtn("tap_mrow", [1, 16], F32, kind="ExternalOutput").ap()
        taps["tap_o4"] = dtn("tap_o4", [128, 512], BF16, kind="ExternalOutput").ap()
        taps["tap_o2"] = dtn("tap_o2", [128, 512], BF16, kind="ExternalOutput").ap()
        taps["tap_rstd"] = dtn("tap_rstd", [1, 512], F32, kind="ExternalOutput").ap()

    C = NS()
    C.qs = Ctr(nc.alloc_semaphore(name="qs"), 16)
    C.qa = Ctr(nc.alloc_semaphore(name="qa"), 16)
    C.qg = Ctr(nc.alloc_semaphore(name="qg"), 16)
    C.qw = Ctr(nc.alloc_semaphore(name="qw"), 16)
    C.qs2 = Ctr(nc.alloc_semaphore(name="qs2"), 16)
    C.qw2 = Ctr(nc.alloc_semaphore(name="qw2"), 16)
    C.cv = Ctr(nc.alloc_semaphore(name="cv"))
    C.cs = Ctr(nc.alloc_semaphore(name="cs"))
    C.ct = Ctr(nc.alloc_semaphore(name="ct"))
    C.cg = Ctr(nc.alloc_semaphore(name="cg"))
    CTRS = {k: getattr(C, k) for k in ("qs", "qa", "qg", "qw", "qs2", "qw2",
                                       "cv", "cs", "ct", "cg")}

    def run_walks(blk, fn):
        base = {k: c.n for k, c in CTRS.items()}
        ends = []
        for me in ENGS:
            for k, c in CTRS.items():
                c.n = base[k]
            getattr(blk, me)(lambda E, me=me: fn(me, E))
            ends.append({k: c.n for k, c in CTRS.items()})
        assert all(e == ends[0] for e in ends), ("walk mismatch", ends)

    def engs(me, E):
        return (E if me == "sync" else None, E if me == "scalar" else None,
                E if me == "vector" else None, E if me == "tensor" else None,
                E if me == "gpsimd" else None)

    glob = ExitStack()
    gsb = lambda n, s, d: glob.enter_context(nc.sbuf_tensor(n, s, d))
    maskT = gsb("maskT", [128, 128], BF16)
    ident_bf = gsb("ident_bf", [128, 128], BF16)
    onesc_bf = gsb("onesc_bf", [128, 1], BF16)
    ones11_bf = gsb("ones11_bf", [1, 1], BF16)
    ones11_f = gsb("ones11_f", [1, 1], F32)
    onesr_f = gsb("onesr_f", [1, 128], F32)
    ones1r_bf = gsb("ones1r_bf", [1, 128], BF16)
    onesr512_bf = gsb("onesr512_bf", [1, 512], BF16)
    cols = gsb("cols", [128, 64], F32)
    mrow = gsb("mrow", [1, 16], F32)
    vpadA = gsb("vpadA", [1, 64], F32)
    vpadB = gsb("vpadB", [1, 64], F32)
    epsln_sb = gsb("epsln_sb", [1, 1], F32)
    epsrms_sb = gsb("epsrms_sb", [1, 1], F32)
    eps64_sb = gsb("eps64_sb", [1, 1], F32)
    nl2_sb = gsb("nl2_sb", [1, 1], F32)
    sclr = gsb("sclr", [1, 32], F32)
    mprt = gsb("mprt", [1, 96], F32)
    mcols = gsb("mcols", [128, 96], F32)
    mcol1 = gsb("mcol1", [128, 1], F32)
    mcol1b = gsb("mcol1b", [128, 1], BF16)

    ab = ExitStack()
    absb = lambda n, s, d: ab.enter_context(nc.sbuf_tensor(n, s, d))
    stream = absb("stream", [128, KC * NT], BF16)
    arow = absb("arow", [1, NT], BF16)
    brow = absb("brow", [1, NT], BF16)
    gam_a = absb("gam_a", [1, D], BF16)
    gam_b = absb("gam_b", [1, D], BF16)
    sft_a = absb("sft_a", [1, D], BF16)
    sft_b = absb("sft_b", [1, D], BF16)
    seg1 = absb("seg1", [1, GT], F32)
    seg2 = absb("seg2", [1, GT], F32)
    segt = absb("segt", [1, GT], F32)
    segu = absb("segu", [1, GT], F32)
    fmqT_sb = absb("fmqT_sb", [128, 128], BF16)
    fmkT_sb = absb("fmkT_sb", [128, 128], BF16)
    fmkbr_bf = absb("fmkbr_bf", [1, 128], BF16)
    rmsr_bf = absb("rmsr_bf", [1, 128], BF16)
    fmqb_sb = absb("fmqb_sb", [128, 1], F32)
    fmkb_sb = absb("fmkb_sb", [128, 1], F32)

    def STRM(m, a, b):
        return stream[:, m * NT + a: m * NT + b]

    def vpads(V, n=2):
        r = 0
        for _ in range(n):
            r = C.cv.inc(V and V.tensor_copy(vpadB[:], vpadA[:]))
        return r

    def finalize_and_allreduce(me, E, pairs, key, wait_cg=0, cols_src=None):
        Y, S, V, T, G = engs(me, E)
        _wt(V, C.cg, wait_cg)
        if cols_src is not None:
            prow, wait_ct0 = cols_src
            for (a, b), slot in pairs:
                C.cv.inc(V and V.tensor_reduce(mcol1[:], mcols[:, a:b], AX.X,
                                               ALU.max))
                vpads(V)
                C.cv.inc(V and V.tensor_copy(mcol1b[:], mcol1[:]))
                nvp = vpads(V)
                _wt(T, C.cv, nvp)
                _wt(T, C.cv, wait_ct0)
                ntp = C.ct.inc(T and T.transpose(prow[:], mcol1b[:], ident_bf[:]))
                _wt(V, C.ct, ntp)
                C.cv.inc(V and V.tensor_reduce(mrow[0:1, slot:slot + 1],
                                               prow[0:1, 0:128], AX.X, ALU.max))
                wait_ct0 = vpads(V)
        else:
            for (a, b), slot in pairs:
                C.cv.inc(V and V.tensor_reduce(mrow[0:1, slot:slot + 1],
                                               mprt[0:1, a:b], AX.X, ALU.max))
        nv = vpads(V)
        lo = min(s for _, s in pairs)
        hi = max(s for _, s in pairs) + 1
        _wt(G, C.cv, nv)
        ng1 = C.qg.inc(G and G.dma_start(out=ar_in[key][0:1, 0:hi - lo],
                                         in_=mrow[0:1, lo:hi]))
        _wt(G, C.qg, ng1)
        ngc = C.cg.inc(G and G.collective_compute(
            "AllReduce", ALU.max, [list(range(8))],
            ins=[ar_in[key][0:1, 0:hi - lo].opt()],
            outs=[ar_out[key][0:1, 0:hi - lo].opt()]))
        _wt(G, C.cg, ngc)
        ng2 = C.qg.inc(G and G.dma_start(out=mrow[0:1, 8 + lo:8 + hi],
                                         in_=ar_out[key][0:1, 0:hi - lo]))
        return ng2

    def make_scale_cols(me, E, gslot, f_slot, s_slot, pcolp, ncol_base, qm=QM_ACT,
                        pcol_free=0):
        Y, S, V, T, G = engs(me, E)
        V and V.tensor_scalar(sclr[0:1, s_slot:s_slot + 1],
                              mrow[0:1, 8 + gslot:9 + gslot],
                              1.0 / qm, 1e-12, ALU.mult, ALU.max)
        vpads(V)
        C.cv.inc(V and V.reciprocal(sclr[0:1, f_slot:f_slot + 1],
                                    sclr[0:1, s_slot:s_slot + 1]))
        nv = vpads(V)
        _wt(T, C.cv, nv)
        _wt(T, C.cv, pcol_free)
        C.ct.inc(T and T.matmul(pcolp[:, 0:1], onesr_f[:],
                                sclr[0:1, f_slot:f_slot + 1], start=True, stop=True,
                                skip_group_check=True))
        nt = C.ct.inc(T and T.matmul(pcolp[:, 1:2], onesr_f[:],
                                     sclr[0:1, s_slot:s_slot + 1], start=True, stop=True,
                                     skip_group_check=True))
        _wt(V, C.ct, nt)
        nv2 = C.cv.inc(V and V.tensor_copy(cols[:, f_slot:f_slot + 2], pcolp[:, 0:2]))
        return nv2

    a_ = ExitStack()
    asb = lambda n, s, d: a_.enter_context(nc.sbuf_tensor(n, s, d))
    apsf = lambda n, s: a_.enter_context(nc.psum_tensor(n, s, F32))
    c_sb = asb("c_sb", [1, D], F32)
    cs_row = asb("cs_row", [1, D], BF16)
    ccols = asb("ccols", [128, 8], BF16)
    wmod_bf = [asb("wmod_bf%d" % i, [128, 3 * D // 2], BF16) for i in range(2)]
    mod_row = asb("mod_row", [1, 6 * D], BF16)
    garow = asb("garow", [1, D], BF16)
    gbrow = asb("gbrow", [1, D], BF16)
    rowt1 = asb("rowt1", [1, D], BF16)
    rowt2 = asb("rowt2", [1, D], BF16)
    xsq_big = asb("xsq_big", [128, 2048], BF16)
    tmp_a = asb("tmp_a", [128, 512], BF16)
    wnat = [asb("wnat%d" % i, [128, IF // 2], F32) for i in range(4)]
    wsc = asb("wsc", [128, 96], F32)
    wfc = asb("wfc", [128, 96], F32)
    wqi = asb("wqi", [128, IF], I32)
    wdq = [asb("wdq%d" % i, [128, IF], BF16) for i in range(2)]
    wstg = [asb("wstg%d" % i, [128, 12 * 128], BF16) for i in range(2)]

    pmod = apsf("pmod", [1, 512])
    pmisc = apsf("pmisc", [128, 64])
    ps1 = apsf("ps1", [1, 512])
    ps2 = apsf("ps2", [1, 512])
    pbc = [apsf("pbc%d" % i, [128, 512]) for i in range(2)]
    ptr = [a_.enter_context(nc.psum_tensor("ptr%d" % i, [128, 128], BF16))
           for i in range(2)]

    def phaseA(me, E):
        Y, S, V, T, G = engs(me, E)

        C.qs.inc(Y and Y.dma_start(out=c_sb[:], in_=c_row))
        C.qs.inc(Y and Y.dma_start(out=fmqb_sb[:], in_=fmq_b))
        nq0 = C.qs.inc(Y and Y.dma_start(out=fmkb_sb[:], in_=fmk_b))
        C.qg.inc(G and G.dma_start(out=fmkbr_bf[:], in_=fmk_b_row))
        C.qg.inc(G and G.dma_start(out=rmsr_bf[:], in_=rms_row))
        C.qg.inc(G and G.dma_start(out=fmqT_sb[:], in_=fmqT))
        C.qg.inc(G and G.dma_start(out=fmkT_sb[:], in_=fmkT))
        G and G.memset(maskT[:], 0.0)
        G and G.affine_select(maskT[:], maskT[:], [[-1, 128]], ALU.is_gt, 1.0,
                              base=0, channel_multiplier=1)
        G and G.memset(ident_bf[:], 1.0)
        ncst = C.cg.inc(G and G.affine_select(ident_bf[:], ident_bf[:], [[-1, 128]],
                                              ALU.is_equal, 0.0, base=0,
                                              channel_multiplier=1))
        V and V.memset(vpadA[:], 0.0)
        V and V.memset(vpadB[:], 0.0)
        V and V.memset(epsln_sb[:], EPS_LN)
        V and V.memset(epsrms_sb[:], EPS_RMS)
        V and V.memset(eps64_sb[:], EPS_RMS * 2.0 ** -64)
        V and V.memset(nl2_sb[:], -32.0 * float(np.log(2.0)))
        V and V.memset(onesc_bf[:], 1.0)
        V and V.memset(ones11_bf[:], 1.0)
        V and V.memset(ones11_f[:], 1.0)
        V and V.memset(onesr_f[:], 1.0)
        V and V.memset(ones1r_bf[:], 1.0)
        nv0 = C.cv.inc(V and V.memset(onesr512_bf[:], 1.0))

        _wt(S, C.qs, nq0)
        ns_silu = C.cs.inc(S and S.activation(cs_row[:], c_sb[:], AF.Silu))
        _wt(T, C.cs, ns_silu)
        _wt(T, C.cv, nv0)
        nt = 0
        for k in range(KC):
            nt = C.ct.inc(T and T.matmul(pmisc[:, k:k + 1],
                                         cs_row[0:1, 128 * k:128 * (k + 1)],
                                         ones11_bf[:], start=True, stop=True,
                                         skip_group_check=True))
        _wt(V, C.ct, nt)
        C.cv.inc(V and V.tensor_copy(ccols[:], pmisc[:, 0:8]))
        nv_cc = vpads(V)
        nv_mod = C.cv.inc(V and V.memset(mod_row[:], 0.0))
        wmb_done = [0, 0]
        for k in range(KC):
            for hf in range(4):
                pi = 4 * k + hf
                pb = pi % 2
                _wt(G, C.ct, wmb_done[pb])
                ngw = C.qg.inc(G and G.dma_start(
                    out=wmod_bf[pb][:],
                    in_=wmod[128 * k:128 * (k + 1), 1536 * hf:1536 * (hf + 1)]))
                _wt(T, C.qg, ngw)
                _wt(T, C.cv, nv_cc)
                for mt in range(3):
                    mtg = 3 * hf + mt
                    _wt(T, C.cv, nv_mod)
                    ntm = C.ct.inc(T and T.matmul(
                        pmod[:], ccols[:, k:k + 1],
                        wmod_bf[pb][:, 512 * mt:512 * (mt + 1)],
                        start=True, stop=True, skip_group_check=True))
                    wmb_done[pb] = ntm
                    _wt(V, C.ct, ntm)
                    nv_mod = C.cv.inc(V and V.tensor_tensor(
                        mod_row[0:1, 512 * mtg:512 * (mtg + 1)],
                        mod_row[0:1, 512 * mtg:512 * (mtg + 1)], pmod[:], ALU.add))
        nx = C.qg.inc(G and G.dma_start(
            out=stream[:].rearrange("p (m t) -> p m t", t=NT),
            in_=xT.rearrange("(m p) t -> p m t", p=128)))
        for h in range(H):
            C.qg.inc(G and G.dma_start(
                out=sc_q[h].rearrange("p (k f) -> p k f", f=128),
                in_=wqT.rearrange("(k p) (h f) -> h p k f", p=128, f=128)[h]))
            C.qg.inc(G and G.dma_start(
                out=sc_k[h].rearrange("p (k f) -> p k f", f=128),
                in_=wkT.rearrange("(k p) (h f) -> h p k f", p=128, f=128)[h]))
        for b in range(2):
            C.qg.inc(G and G.dma_start(
                out=sc_v[b].rearrange("p (k f) -> p k f", f=512),
                in_=wvT.rearrange("(k p) (b f) -> b p k f", p=128, f=512)[b]))
        for m in range(KC):
            C.qg.inc(G and G.dma_start(
                out=sc_o[m].rearrange("p (h f) -> p h f", f=128),
                in_=woT.rearrange("(h p) (m f) -> m p h f", p=128, f=128)[m]))

        if _DEBUG_TAPS:
            _wt(Y, C.cv, nv_mod)
            C.qs.inc(Y and Y.dma_start(out=taps["tap_mod"], in_=mod_row[:]))

        sa_s = mod_row[0:1, 0 * D:1 * D]
        ca_s = mod_row[0:1, 1 * D:2 * D]
        ga_s = mod_row[0:1, 2 * D:3 * D]
        sb_s = mod_row[0:1, 3 * D:4 * D]
        cb_s = mod_row[0:1, 4 * D:5 * D]
        gb_s = mod_row[0:1, 5 * D:6 * D]
        V and V.tensor_scalar(gam_a[:], ca_s, 1.0, None, ALU.add)
        V and V.tensor_scalar(gam_b[:], cb_s, 1.0, None, ALU.add)
        V and V.tensor_copy(sft_a[:], sa_s)
        V and V.tensor_copy(sft_b[:], sb_s)
        for src, dstrow in ((ga_s, garow), (gb_s, gbrow)):
            V and V.tensor_scalar(rowt1[:], src, -1.0, None, ALU.mult)
            V and V.tensor_tensor(rowt2[:], rowt1[:], src, ALU.max)
            V and V.tensor_scalar(dstrow[:], rowt2[:], 0.25, None, ALU.max)
            V and V.tensor_scalar(rowt1[:], src, 0.0, None, ALU.is_ge)
            V and V.tensor_scalar(rowt2[:], rowt1[:], 2.0, 1.0, ALU.mult, ALU.subtract)
            V and V.tensor_tensor(dstrow[:], rowt2[:], dstrow[:], ALU.mult)
        V and V.tensor_scalar(cols[:, 28:29], fmqb_sb[:], -LN16, None, ALU.add)
        V and V.tensor_scalar(cols[:, 29:30], fmqb_sb[:], -1.0, -LN16, ALU.mult, ALU.add)
        V and V.tensor_copy(cols[:, 30:31], fmkb_sb[:])
        nv_pp = C.cv.inc(V and V.tensor_scalar(cols[:, 31:32], fmkb_sb[:], -1.0, None,
                                               ALU.mult))
        _wt(T, C.cv, nv_pp)
        nt_g = 0
        for k in range(KC):
            C.ct.inc(T and T.matmul(pmisc[:, 16 + k:17 + k],
                                    garow[0:1, 128 * k:128 * (k + 1)], ones11_bf[:],
                                    start=True, stop=True, skip_group_check=True))
            nt_g = C.ct.inc(T and T.matmul(pmisc[:, 24 + k:25 + k],
                                           gbrow[0:1, 128 * k:128 * (k + 1)], ones11_bf[:],
                                           start=True, stop=True, skip_group_check=True))
        _wt(V, C.ct, nt_g)
        nv_gc0 = C.cv.inc(V and V.tensor_copy(cols[:, 0:16], pmisc[:, 16:32]))
        _wt(T, C.cv, nv_gc0)
        nt_g2 = 0
        for vi, row in enumerate((gam_a, sft_a, gam_b, sft_b)):
            for k in range(KC):
                nt_g2 = C.ct.inc(T and T.matmul(
                    pmisc[:, 8 * vi + k:8 * vi + k + 1],
                    row[0:1, 128 * k:128 * (k + 1)], ones11_bf[:],
                    start=True, stop=True, skip_group_check=True))
        _wt(V, C.ct, nt_g2)
        C.cv.inc(V and V.tensor_copy(cols[:, 32:64], pmisc[:, 0:32]))

        _wt(T, C.cg, ncst)
        wstg_dma = [0, 0]
        wnat_free = [0, 0, 0, 0]
        wdq_used = [0, 0]
        ptr_free = [0, 0]
        jobs = []
        for wsrc, scdst, ntile, width in ((w_up, sc_up, IC, D), (w_gate, sc_gt, IC, D),
                                          (w_down, sc_dn, KC, IF)):
            for it in range(ntile):
                if width <= IF // 2:
                    jobs.append((wsrc, scdst, it, width, 0))
                else:
                    jobs.append((wsrc, scdst, it, width // 2, 0))
                    jobs.append((wsrc, scdst, it, width // 2, 1))
        for ji, (wsrc, scdst, it, width, hv) in enumerate(jobs):
            if True:
                nblk = width // 32
                nkc = width // 128
                bi = ji % 4
                bq = ji % 2
                buf = wnat[bi]
                LE, lq = Y, C.qs
                _wt(LE, C.cv, wnat_free[bi])
                nload = lq.inc(LE and LE.dma_start(
                    out=buf[:, 0:width],
                    in_=wsrc[128 * it:128 * (it + 1),
                             hv * width:(hv + 1) * width]))
                _wt(V, lq, nload)
                w3 = buf[:, 0:width].rearrange("p (b c) -> p b c", c=32)
                C.cv.inc(V and V.tensor_reduce(
                    wsc[:, 0:nblk].unsqueeze(2), w3, AX.X, ALU.max,
                    apply_absolute_value=True))
                vpads(V)
                V and V.tensor_scalar(wsc[:, 0:nblk], wsc[:, 0:nblk], 1.0 / QM_W, 1e-12,
                                      ALU.mult, ALU.max)
                vpads(V)
                V and V.reciprocal(wfc[:, 0:nblk], wsc[:, 0:nblk])
                vpads(V)
                fb3 = wfc[:, 0:nblk].unsqueeze(2).broadcast_to([128, nblk, 32])
                sb3 = wsc[:, 0:nblk].unsqueeze(2).broadcast_to([128, nblk, 32])
                _wt(V, C.ct, wdq_used[bq])
                V and V.tensor_tensor(
                    wqi[:, 0:width].rearrange("p (b c) -> p b c", c=32), w3, fb3,
                    ALU.mult)
                nv_dq = C.cv.inc(V and V.tensor_tensor(
                    wdq[bq][:, 0:width].rearrange("p (b c) -> p b c", c=32),
                    wqi[:, 0:width].rearrange("p (b c) -> p b c", c=32), sb3, ALU.mult))
                wnat_free[bi] = nv_dq
                _wt(T, C.cv, nv_dq)
                _wt(T, C.qw2, wstg_dma[bq])
                nstg = 0
                ntt_last = 0
                for k in range(nkc):
                    _wt(T, C.cv, ptr_free[k % 2])
                    ntt_last = C.ct.inc(T and T.transpose(
                        ptr[k % 2][:], wdq[bq][:, 128 * k:128 * (k + 1)], ident_bf[:]))
                    _wt(V, C.ct, ntt_last)
                    nstg = C.cv.inc(V and V.tensor_copy(
                        wstg[bq][:, 128 * k:128 * (k + 1)], ptr[k % 2][:]))
                    ptr_free[k % 2] = nstg
                wdq_used[bq] = ntt_last
                WE, wq = S, C.qw2
                _wt(WE, C.cv, nstg)
                wstg_dma[bq] = wq.inc(WE and WE.dma_start(
                    out=scdst[it][:, hv * nkc * 128:(hv + 1) * nkc * 128],
                    in_=wstg[bq][:, 0:nkc * 128]))
                if _DEBUG_TAPS and wi == 0 and it == 0:
                    C.qs.inc(Y and Y.dma_start(out=taps["tap_wup"],
                                               in_=wstg[bq][:, 0:nkc * 128]))

        _wt(T, C.qg, nx)
        _wt(S, C.qg, nx)
        sq_used = 0
        for g in range(NG):
            t0 = g * GT
            nv_seg_prev = 0
            for tt in range(2):
                ta = t0 + 512 * tt
                strm3 = stream[:].rearrange("p (m t) -> p m t", t=NT)[:, :, ta:ta + 512]
                nt_st = 0
                for hq in range(2):
                    _wt(S, C.ct, sq_used)
                    ns_sq = C.cs.inc(S and S.activation(
                        xsq_big[:].rearrange("p (m t) -> p m t", t=512),
                        strm3[:, 4 * hq:4 * (hq + 1), :], AF.Square))
                    for mq in range(4):
                        m = 4 * hq + mq
                        _wt(T, C.cs, ns_sq)
                        if m == 0:
                            _wt(T, C.cv, nv_seg_prev)
                        C.ct.inc(T and T.matmul(ps1[:], onesc_bf[:],
                                                STRM(m, ta, ta + 512),
                                                start=(m == 0), stop=(m == KC - 1),
                                                skip_group_check=True))
                        nt_st = C.ct.inc(T and T.matmul(
                            ps2[:], onesc_bf[:], xsq_big[:, 512 * mq:512 * (mq + 1)],
                            start=(m == 0), stop=(m == KC - 1),
                            skip_group_check=True))
                    sq_used = nt_st
                _wt(V, C.ct, nt_st)
                V and V.tensor_copy(seg1[0:1, 512 * tt:512 * (tt + 1)], ps1[:])
                nv_seg_prev = C.cv.inc(V and V.tensor_copy(
                    seg2[0:1, 512 * tt:512 * (tt + 1)], ps2[:]))
            V and V.tensor_scalar(segt[:], seg1[:], 1.0 / D, None, ALU.mult)
            vpads(V)
            V and V.scalar_tensor_tensor(segu[:], segt[:], -float(D), segt[:],
                                         ALU.mult, ALU.mult)
            vpads(V)
            C.cv.inc(V and V.tensor_tensor(seg2[:], seg2[:], segu[:], ALU.add))
            nv_var = vpads(V)
            _wt(S, C.cv, nv_var)
            S and S.activation(seg1[:], seg2[:], AF.Ln, bias=epsln_sb[:], scale=1.0 / D)
            ns_rstd = C.cs.inc(S and S.activation(seg1[:], seg1[:], AF.Exp, scale=-0.5))
            _wt(V, C.cs, ns_rstd)
            V and V.tensor_copy(arow[0:1, t0:t0 + GT], seg1[:])
            V and V.scalar_tensor_tensor(segu[:], segt[:], -1.0, seg1[:], ALU.mult,
                                         ALU.mult)
            vpads(V)
            C.cv.inc(V and V.tensor_copy(brow[0:1, t0:t0 + GT], segu[:]))
            nv_rows = vpads(V)
            _wt(T, C.cv, nv_rows)
            for tt in range(2):
                ta = t0 + 512 * tt
                pa, pb = pbc[0], pbc[1]
                _wt(T, C.cv, getattr(phaseA, "_dummy", 0))
                C.ct.inc(T and T.matmul(pa[:], ones1r_bf[:],
                                        arow[0:1, ta:ta + 512], start=True, stop=True,
                                        skip_group_check=True))
                nt_bc = C.ct.inc(T and T.matmul(pb[:], ones1r_bf[:],
                                               brow[0:1, ta:ta + 512], start=True,
                                               stop=True, skip_group_check=True))
                _wt(V, C.ct, nt_bc)
                nv_h = 0
                for m in range(KC):
                    V and V.tensor_tensor(tmp_a[:], STRM(m, ta, ta + 512), pa[:],
                                          ALU.mult)
                    V and V.tensor_tensor(tmp_a[:], tmp_a[:], pb[:], ALU.add)
                    nv_h = C.cv.inc(V and V.tensor_scalar(
                        STRM(m, ta, ta + 512), tmp_a[:], cols[:, 32 + m:33 + m],
                        cols[:, 40 + m:41 + m], ALU.mult, ALU.add))
                _wt(T, C.cv, nv_h)
        if _DEBUG_TAPS:
            _wt(Y, C.cv, C.cv.val)
            C.qs.inc(Y and Y.dma_start(out=taps["tap_h"], in_=stream[:]))

    with nc.Block() as blk:
        run_walks(blk, phaseA)
    a_.close()

    nA = {k: c.val for k, c in CTRS.items()}

    b_ = ExitStack()
    bsb = lambda n, s, d: b_.enter_context(nc.sbuf_tensor(n, s, d))
    bpsf = lambda n, s: b_.enter_context(nc.psum_tensor(n, s, F32))
    q_raw = [bsb("q_raw%d" % i, [128, GT], BF16) for i in range(2)]
    k_raw = [bsb("k_raw%d" % i, [128, GT], BF16) for i in range(2)]
    qm0 = [bsb("qm0_%d" % i, [128, GT], BF16) for i in range(2)]
    qm1 = [bsb("qm1_%d" % i, [128, GT], BF16) for i in range(2)]
    km0 = [bsb("km0_%d" % i, [128, GT], BF16) for i in range(2)]
    km1 = [bsb("km1_%d" % i, [128, GT], BF16) for i in range(2)]
    ktok = [bsb("ktok%d" % i, [128, 2 * GT], BF16) for i in range(2)]
    v_g = bsb("v_g", [128, 8 * D], BF16)
    on_g = bsb("on_g", [128, H * GT], BF16)
    o4 = bsb("o4", [128, 512], BF16)
    o2sb = bsb("o2sb", [128, 512], BF16)
    scsb = [bsb("scsb%d" % i, [128, 128], BF16) for i in range(2)]
    Ssb = bsb("Ssb", [128, 256], BF16)
    Spark = bsb("Spark", [128, 256], BF16)
    Spark2 = [bsb("Spark2_%d" % i, [128, 256], BF16) for i in range(2)]
    wslab = [bsb("wslab%d" % i, [128, KC * 128], BF16) for i in range(2)]
    wvlab = bsb("wvlab", [128, KC * 512], BF16)
    wolab = [bsb("wolab%d" % i, [128, H * 128], BF16) for i in range(2)]
    xres = [bsb("xres%d" % i, [128, 512], F32) for i in range(2)]
    rstd_row = bsb("rstd_row", [1, 512], F32)
    rstd_bf = bsb("rstd_bf", [1, 512], BF16)
    h2stg = [bsb("h2stg%d" % i, [128, 512], BF16) for i in range(2)]
    tmpb = bsb("tmpb", [128, 512], BF16)
    xsqB = [bsb("xsqB%d" % i, [128, 512], BF16) for i in range(2)]

    pS = [bpsf("pS%d" % i, [128, 128]) for i in range(2)]
    po = bpsf("po", [128, 128])[:]
    pscb = bpsf("pscb", [128, 256])
    psc = [pscb[:, 0:128], pscb[:, 128:256]]
    psq = bpsf("psq", [1, 512])
    pbcB = bpsf("pbcB", [128, 512])
    pe_ = [bpsf("pe%d" % i, [128, 512]) for i in range(2)]

    def phaseB(me, E):
        Y, S, V, T, G = engs(me, E)
        for X in (Y, S, V, T, G):
            if X:
                for k, v in nA.items():
                    X.wait_ge(CTRS[k].sem, v)

        st = NS()
        st.pe_free = [0, 0]
        st.pe_free_cs = [0, 0]
        st.po_free_v = 0
        st.po_free_s = 0
        st.psc_free = [0, 0]
        st.nt_sc = [0, 0]
        st.psq_free = 0
        st.pbc_free = 0
        st.pending = None
        st.nv_rb = 0
        st.scsb_used2 = [0, 0]
        st.Ssb_mark = 0
        st.S_last_mm = 0
        st.restore_mark = 0
        st.wslab_used = [0, 0]
        st.wvlab_used = 0
        st.wolab_used = [0, 0]
        st.ong_used = 0
        st.park_dma = 0
        st.spark_dma = [0, 0]
        st.sp2_used = [0, 0]
        st.xres_used = [0, 0]
        st.m1slot = 0
        st.h2_dma = [0, 0]
        st.h2_red = [0, 0]
        st.xsqB_used = [0, 0]
        st.ktok_done = [0, 0]

        def use_pe(tt):
            _wt(T, C.cv, st.pe_free[tt])
            _wt(T, C.cs, st.pe_free_cs[tt])

        def make_head_pieces(g, h, hp):
            t0 = g * GT
            pieces = []

            def mk_wdma(src):
                def p():
                    _wt(S, C.ct, st.wslab_used[hp])
                    nw = C.qa.inc(S and S.dma_start(out=wslab[hp][:], in_=src))
                    st.wslab_dma = nw
                return p

            def mk_proj(which, tt):
                def p():
                    raw = (q_raw if which == "q" else k_raw)[hp]
                    pe = pe_[tt]
                    use_pe(tt)
                    _wt(T, C.qa, st.wslab_dma)
                    nt_p = 0
                    for k in range(KC):
                        nt_p = C.ct.inc(T and T.matmul(
                            pe[:], wslab[hp][:, 128 * k:128 * (k + 1)],
                            STRM(k, t0 + 512 * tt, t0 + 512 * (tt + 1)),
                            start=(k == 0), stop=(k == KC - 1),
                            skip_group_check=True))
                    if tt == 1:
                        st.wslab_used[hp] = nt_p
                    _wt(V, C.ct, nt_p)
                    nv_r = C.cv.inc(V and V.tensor_copy(
                        raw[:, 512 * tt:512 * (tt + 1)], pe[:]))
                    st.pe_free[tt] = nv_r
                    st.pe_free_cs[tt] = 0
                return p

            def mk_map(which, tt):
                def p():
                    raw = (q_raw if which == "q" else k_raw)[hp]
                    m0 = (qm0 if which == "q" else km0)[hp]
                    m1 = (qm1 if which == "q" else km1)[hp]
                    fmT = fmqT_sb if which == "q" else fmkT_sb
                    bp = cols[:, 28:29] if which == "q" else cols[:, 30:31]
                    bm = cols[:, 29:30] if which == "q" else cols[:, 31:32]
                    pe = pe_[tt]
                    use_pe(tt)
                    nt_y = C.ct.inc(T and T.matmul(pe[:], fmT[:],
                                                   raw[:, 512 * tt:512 * (tt + 1)],
                                                   start=True, stop=True,
                                                   skip_group_check=True))
                    _wt(S, C.ct, nt_y)
                    S and S.activation(m0[:, 512 * tt:512 * (tt + 1)], pe[:], AF.Exp,
                                       bias=bp, scale=1.0)
                    ns_e = C.cs.inc(S and S.activation(
                        m1[:, 512 * tt:512 * (tt + 1)], pe[:], AF.Exp, bias=bm,
                        scale=-1.0))
                    st.pe_free[tt] = 0
                    st.pe_free_cs[tt] = ns_e
                return p

            def mk_ktok(tt):
                def p():
                    pe = pe_[tt]
                    use_pe(tt)
                    nt_k = 0
                    for sl in range(4):
                        cc = tt * 4 + sl
                        C.ct.inc(T and T.matmul(pe[:, 128 * sl:128 * (sl + 1)],
                                                ones1r_bf[:], fmkbr_bf[:],
                                                start=True, stop=False,
                                                skip_group_check=True))
                        nt_k = C.ct.inc(T and T.matmul(
                            pe[:, 128 * sl:128 * (sl + 1)],
                            k_raw[hp][:, 128 * cc:128 * (cc + 1)], fmkT_sb[:],
                            start=False, stop=True, skip_group_check=True))
                    _wt(S, C.ct, nt_k)
                    dstp = ktok[hp][:].rearrange("p (c e) -> p c e", e=256)
                    srcp = pe[:].rearrange("p (c e) -> p c e", e=128)
                    S and S.activation(dstp[:, 4 * tt:4 * (tt + 1), 0:128], srcp,
                                       AF.Exp, scale=1.0)
                    ns_k = C.cs.inc(S and S.activation(
                        dstp[:, 4 * tt:4 * (tt + 1), 128:256], srcp, AF.Exp,
                        scale=-1.0))
                    st.pe_free[tt] = 0
                    st.pe_free_cs[tt] = ns_k
                    st.ktok_done[hp] = ns_k
                return p

            pieces.append(mk_wdma(sc_q[h]))
            pieces.append(mk_proj("q", 0))
            pieces.append(mk_proj("q", 1))
            pieces.append(mk_map("q", 0))
            pieces.append(mk_map("q", 1))
            pieces.append(mk_wdma(sc_k[h]))
            pieces.append(mk_proj("k", 0))
            pieces.append(mk_proj("k", 1))
            pieces.append(mk_map("k", 0))
            pieces.append(mk_map("k", 1))
            pieces.append(mk_ktok(0))
            pieces.append(mk_ktok(1))
            return pieces

        def emit_rms_apply():
            h_, c0 = st.pending
            _wt(T, C.cv, st.nv_rb)
            _wt(T, C.cv, st.pbc_free)
            nt_bc = C.ct.inc(T and T.matmul(pbcB[:], rmsr_bf[:], rstd_bf[:],
                                            start=True, stop=True,
                                            skip_group_check=True))
            _wt(V, C.ct, nt_bc)
            _wt(V, C.ct, st.ong_used)
            st.pbc_free = C.cv.inc(V and V.tensor_tensor(
                on_g[:, GT * h_ + c0:GT * h_ + c0 + 512], o4[:], pbcB[:], ALU.mult))
            st.pending = None

        def emit_scores(ci, hp, sl):
            if ci == 0:
                _wt(T, C.cs, st.ktok_done[hp])
            _wt(T, C.cv, st.psc_free[sl])
            C.ct.inc(T and T.matmul(psc[sl], km0[hp][:, 128 * ci:128 * (ci + 1)],
                                    qm0[hp][:, 128 * ci:128 * (ci + 1)],
                                    start=True, stop=False, skip_group_check=True))
            st.nt_sc[sl] = C.ct.inc(T and T.matmul(
                psc[sl], km1[hp][:, 128 * ci:128 * (ci + 1)],
                qm1[hp][:, 128 * ci:128 * (ci + 1)],
                start=False, stop=True, skip_group_check=True))

        def scan_chunk(g, h, ci, hp):
            first = (ci == 0 and g == 0)
            sl = ci % 2
            if ci == 0:
                emit_scores(0, hp, 0)
            if ci + 1 < GC:
                emit_scores(ci + 1, hp, (ci + 1) % 2)
            _wt(V, C.ct, st.nt_sc[sl])
            _wt(V, C.ct, st.scsb_used2[sl])
            nv_sc = C.cv.inc(V and V.tensor_tensor(scsb[sl][:], psc[sl], maskT[:],
                                                   ALU.mult))
            st.psc_free[sl] = nv_sc
            vsl = v_g[:, 1024 * ci + 128 * h:1024 * ci + 128 * (h + 1)]
            _wt(T, C.cv, nv_sc)
            _wt(T, C.cv, st.po_free_v)
            _wt(T, C.cs, st.po_free_s)
            nt_o = C.ct.inc(T and T.matmul(po, vsl, scsb[sl][:],
                                           start=True, stop=first,
                                           skip_group_check=True))
            st.scsb_used2[sl] = nt_o
            if not first:
                _wt(T, C.cv, st.Ssb_mark)
                C.ct.inc(T and T.matmul(po, Ssb[:, 0:128],
                                        qm0[hp][:, 128 * ci:128 * (ci + 1)],
                                        start=False, stop=False,
                                        skip_group_check=True))
                nt_o = C.ct.inc(T and T.matmul(po, Ssb[:, 128:256],
                                               qm1[hp][:, 128 * ci:128 * (ci + 1)],
                                               start=False, stop=True,
                                               skip_group_check=True))
            ksl = ktok[hp][:, 256 * ci:256 * (ci + 1)]
            s_start = (ci == 0 and g == 0)
            C.ct.inc(T and T.matmul(pS[0][:], ksl[:, 0:128], vsl,
                                    start=s_start, stop=False, skip_group_check=True))
            nt_S = C.ct.inc(T and T.matmul(pS[1][:], ksl[:, 128:256], vsl,
                                           start=s_start, stop=False,
                                           skip_group_check=True))
            st.S_last_mm = nt_S
            _wt(V, C.ct, nt_S)
            V and V.tensor_copy(Ssb[:, 0:128], pS[0][:])
            C.cv.inc(V and V.tensor_copy(Ssb[:, 128:256], pS[1][:]))
            st.Ssb_mark = vpads(V, 1)
            if st.pending is not None:
                emit_rms_apply()
            _wt(S, C.ct, nt_o)
            st.po_free_s = C.cs.inc(S and S.activation(
                o2sb[:, 128 * (ci % 4):128 * (ci % 4 + 1)], po, AF.Square,
                scale=2.0 ** -32))
            _wt(V, C.ct, nt_o)
            st.po_free_v = C.cv.inc(V and V.tensor_copy(
                o4[:, 128 * (ci % 4):128 * (ci % 4 + 1)], po))
            if ci % 4 == 3:
                _wt(T, C.cs, st.po_free_s)
                _wt(T, C.cs, st.psq_free)
                nt_sq = 0
                for cj in range(4):
                    nt_sq = C.ct.inc(T and T.matmul(
                        psq[0:1, 128 * cj:128 * (cj + 1)], onesc_bf[:],
                        o2sb[:, 128 * cj:128 * (cj + 1)], start=True, stop=True,
                        skip_group_check=True))
                _wt(S, C.ct, nt_sq)
                S and S.activation(rstd_row[:], psq[0:1, :], AF.Ln,
                                   bias=eps64_sb[:], scale=1.0 / 128.0)
                ns_r = C.cs.inc(S and S.activation(
                    rstd_row[:], rstd_row[:], AF.Exp, bias=nl2_sb[:], scale=-0.5))
                st.psq_free = ns_r
                _wt(V, C.cs, ns_r)
                C.cv.inc(V and V.tensor_copy(rstd_bf[:], rstd_row[:]))
                st.nv_rb = vpads(V, 1)
                st.pending = (h, (ci - 3) * 128)

        for g in range(NG):
            t0 = g * GT
            for half in range(2):
                _wt(S, C.ct, st.wvlab_used)
                nwv = C.qa.inc(S and S.dma_start(out=wvlab[:], in_=sc_v[half]))
                _wt(T, C.qa, nwv)
                for ts in range(8):
                    tt = ts % 2
                    pe = pe_[tt]
                    use_pe(tt)
                    nt_v = 0
                    for k in range(KC):
                        nt_v = C.ct.inc(T and T.matmul(
                            pe[:], STRM(k, t0 + 128 * ts, t0 + 128 * (ts + 1)),
                            wvlab[:, 512 * k:512 * (k + 1)],
                            start=(k == 0), stop=(k == KC - 1),
                            skip_group_check=True))
                    _wt(V, C.ct, nt_v)
                    nv_v = C.cv.inc(V and V.tensor_copy(
                        v_g[:, 1024 * ts + 512 * half:1024 * ts + 512 * (half + 1)],
                        pe[:]))
                    st.pe_free[tt] = nv_v
                    st.pe_free_cs[tt] = 0
                st.wvlab_used = C.ct.val
            for h in range(H):
                hidx = g * H + h
                hp = hidx % 2
                if hidx == 0:
                    for p in make_head_pieces(0, 0, 0):
                        p()
                if g > 0:
                    sp2 = Spark2[hidx % 2]
                    _wt(T, C.qs, st.spark_dma[hidx % 2])
                    _wt(T, C.cv, st.Ssb_mark)
                    C.ct.inc(T and T.matmul(pS[0][:], ident_bf[:], sp2[:, 0:128],
                                            start=True, stop=False,
                                            skip_group_check=True))
                    st.restore_mark = C.ct.inc(T and T.matmul(
                        pS[1][:], ident_bf[:], sp2[:, 128:256],
                        start=True, stop=False, skip_group_check=True))
                    st.sp2_used[hidx % 2] = st.restore_mark
                    _wt(V, C.ct, st.restore_mark)
                    V and V.tensor_copy(Ssb[:, 0:128], pS[0][:])
                    C.cv.inc(V and V.tensor_copy(Ssb[:, 128:256], pS[1][:]))
                    st.Ssb_mark = vpads(V, 1)
                if hidx + 1 < NG * H and hidx + 1 >= H:
                    slot = (hidx + 1) % 2
                    _wt(Y, C.ct, st.sp2_used[slot])
                    st.spark_dma[slot] = C.qs.inc(Y and Y.dma_start(
                        out=Spark2[slot][:], in_=sp_S[(hidx + 1) % H]))
                if hidx + 1 < NG * H:
                    gn, hn = divmod(hidx + 1, H)
                    nxt_pieces = make_head_pieces(gn, hn, (hidx + 1) % 2)
                else:
                    nxt_pieces = []
                while nxt_pieces:
                    nxt_pieces.pop(0)()
                for ci in range(GC):
                    scan_chunk(g, h, ci, hp)
                if st.pending is not None:
                    emit_rms_apply()
                if g < NG - 1:
                    _wt(V, C.ct, st.S_last_mm)
                    _wt(V, C.qs, st.park_dma)
                    V and V.tensor_copy(Spark[:, 0:128], pS[0][:])
                    nv_pk = C.cv.inc(V and V.tensor_copy(Spark[:, 128:256], pS[1][:]))
                    _wt(Y, C.cv, nv_pk)
                    st.park_dma = C.qs.inc(Y and Y.dma_start(out=sp_S[h], in_=Spark[:]))
                    st.Ssb_mark = nv_pk
            if _DEBUG_TAPS and g == 0:
                _wt(Y, C.cv, st.pbc_free)
                _wt(Y, C.cs, max(st.ktok_done))
                C.qs.inc(Y and Y.dma_start(out=taps["tap_ong"], in_=on_g[:]))
                C.qs.inc(Y and Y.dma_start(out=taps["tap_vg"], in_=v_g[:]))
                C.qs.inc(Y and Y.dma_start(
                    out=taps["tap_qm"].rearrange("p (a t) -> p a t", a=2)[:, 0],
                    in_=qm0[0][:]))
                C.qs.inc(Y and Y.dma_start(
                    out=taps["tap_qm"].rearrange("p (a t) -> p a t", a=2)[:, 1],
                    in_=qm1[0][:]))
                C.qs.inc(Y and Y.dma_start(
                    out=taps["tap_km"].rearrange("p (a t) -> p a t", a=2)[:, 0],
                    in_=km0[0][:]))
                C.qs.inc(Y and Y.dma_start(
                    out=taps["tap_km"].rearrange("p (a t) -> p a t", a=2)[:, 1],
                    in_=km1[0][:]))
                C.qs.inc(Y and Y.dma_start(out=taps["tap_ktok"], in_=ktok[0][:]))
                C.qs.inc(Y and Y.dma_start(out=taps["tap_Ssb"], in_=Ssb[:]))
                C.qs.inc(Y and Y.dma_start(out=taps["tap_o4"], in_=o4[:]))
                C.qs.inc(Y and Y.dma_start(out=taps["tap_o2"], in_=o2sb[:]))
                C.qs.inc(Y and Y.dma_start(out=taps["tap_rstd"], in_=rstd_row[:]))
            ong_new = 0
            _wt(T, C.cv, st.pbc_free)
            for m in range(KC):
                _wt(S, C.ct, st.wolab_used[m % 2])
                nwo = C.qa.inc(S and S.dma_start(out=wolab[m % 2][:], in_=sc_o[m]))
                _wt(T, C.qa, nwo)
                for t2 in range(2):
                    pe = pe_[t2]
                    use_pe(t2)
                    _wt(Y, C.cv, st.xres_used[t2])
                    nxr = C.qs.inc(Y and Y.dma_start(
                        out=xres[t2][:],
                        in_=xT[128 * m:128 * (m + 1),
                               t0 + 512 * t2:t0 + 512 * (t2 + 1)]))
                    nt_op = 0
                    for hh in range(H):
                        nt_op = C.ct.inc(T and T.matmul(
                            pe[:], wolab[m % 2][:, 128 * hh:128 * (hh + 1)],
                            on_g[:, GT * hh + 512 * t2:GT * hh + 512 * (t2 + 1)],
                            start=(hh == 0), stop=(hh == H - 1),
                            skip_group_check=True))
                    ong_new = nt_op
                    _wt(V, C.ct, nt_op)
                    _wt(V, C.qs, nxr)
                    nv_x2 = C.cv.inc(V and V.scalar_tensor_tensor(
                        STRM(m, t0 + 512 * t2, t0 + 512 * (t2 + 1)),
                        pe[:], cols[:, m:m + 1], xres[t2][:], ALU.mult, ALU.add))
                    st.xres_used[t2] = nv_x2
                    st.pe_free[t2] = nv_x2
                    st.pe_free_cs[t2] = 0
                st.wolab_used[m % 2] = C.ct.val
            st.ong_used = ong_new
            nv_x2g = C.cv.val

            nv_seg_prev = 0
            for tt in range(2):
                ta = t0 + 512 * tt
                nt_st = 0
                for m in range(KC):
                    _wt(S, C.cv, nv_x2g)
                    _wt(S, C.ct, st.xsqB_used[m % 2])
                    ns_sq = C.cs.inc(S and S.activation(xsqB[m % 2][:],
                                                        STRM(m, ta, ta + 512),
                                                        AF.Square))
                    _wt(T, C.cs, ns_sq)
                    if m == 0:
                        _wt(T, C.cv, nv_x2g)
                        _wt(T, C.cv, nv_seg_prev)
                        _wt(T, C.cs, st.psq_free)
                        _wt(T, C.cv, st.pbc_free)
                    C.ct.inc(T and T.matmul(psq[0:1, :], onesc_bf[:], STRM(m, ta, ta + 512),
                                            start=(m == 0), stop=(m == KC - 1),
                                            skip_group_check=True))
                    nt_st = C.ct.inc(T and T.matmul(pbcB[0:1, :], onesc_bf[:],
                                                    xsqB[m % 2][:],
                                                    start=(m == 0), stop=(m == KC - 1),
                                                    skip_group_check=True))
                    st.xsqB_used[m % 2] = nt_st
                _wt(V, C.ct, nt_st)
                V and V.tensor_copy(seg1[0:1, 512 * tt:512 * (tt + 1)], psq[0:1, :])
                nv_seg_prev = C.cv.inc(V and V.tensor_copy(
                    seg2[0:1, 512 * tt:512 * (tt + 1)], pbcB[0:1, :]))
            st.psq_free = 0
            st.pbc_free = nv_seg_prev
            V and V.tensor_scalar(segt[:], seg1[:], 1.0 / D, None, ALU.mult)
            vpads(V)
            V and V.scalar_tensor_tensor(segu[:], segt[:], -float(D), segt[:],
                                         ALU.mult, ALU.mult)
            vpads(V)
            C.cv.inc(V and V.tensor_tensor(seg2[:], seg2[:], segu[:], ALU.add))
            nv_var = vpads(V)
            _wt(S, C.cv, nv_var)
            S and S.activation(seg1[:], seg2[:], AF.Ln, bias=epsln_sb[:], scale=1.0 / D)
            ns_rstd = C.cs.inc(S and S.activation(seg1[:], seg1[:], AF.Exp, scale=-0.5))
            _wt(V, C.cs, ns_rstd)
            V and V.tensor_copy(arow[0:1, t0:t0 + GT], seg1[:])
            V and V.scalar_tensor_tensor(segu[:], segt[:], -1.0, seg1[:], ALU.mult,
                                         ALU.mult)
            vpads(V)
            C.cv.inc(V and V.tensor_copy(brow[0:1, t0:t0 + GT], segu[:]))
            nv_rows = vpads(V)
            _wt(T, C.cv, nv_rows)
            for m in range(KC):
                for tt in range(2):
                    ta = t0 + 512 * tt
                    bi = (m * 2 + tt) % 2
                    pe = pe_[bi]
                    use_pe(bi)
                    _wt(T, C.cv, st.pbc_free)
                    C.ct.inc(T and T.matmul(pbcB[:], gam_b[0:1, 128 * m:128 * (m + 1)],
                                            arow[0:1, ta:ta + 512], start=True,
                                            stop=True, skip_group_check=True))
                    C.ct.inc(T and T.matmul(pe[:], gam_b[0:1, 128 * m:128 * (m + 1)],
                                            brow[0:1, ta:ta + 512], start=True,
                                            stop=False, skip_group_check=True))
                    nt_bc = C.ct.inc(T and T.matmul(
                        pe[:], sft_b[0:1, 128 * m:128 * (m + 1)], onesr512_bf[:],
                        start=False, stop=True, skip_group_check=True))
                    _wt(V, C.ct, nt_bc)
                    _wt(V, C.qs, st.h2_dma[bi])
                    _wt(V, C.cg, st.h2_red[bi])
                    V and V.tensor_tensor(tmpb[:], STRM(m, ta, ta + 512), pbcB[:],
                                          ALU.mult)
                    nv_h2 = C.cv.inc(V and V.tensor_tensor(h2stg[bi][:], tmpb[:],
                                                           pe[:], ALU.add))
                    st.pe_free[bi] = nv_h2
                    st.pe_free_cs[bi] = 0
                    st.pbc_free = nv_h2
                    _wt(G, C.cv, nv_h2)
                    st.h2_red[bi] = C.cg.inc(G and G.tensor_reduce(
                        mprt[0:1, st.m1slot:st.m1slot + 1], h2stg[bi][:],
                        AX.XYZWC, ALU.abs_max))
                    st.m1slot += 1
                    _wt(Y, C.cv, nv_h2)
                    st.h2_dma[bi] = C.qs.inc(Y and Y.dma_start(
                        out=sp_h2[m][:, ta:ta + 512], in_=h2stg[bi][:]))
                    C.qs.inc(Y and Y.dma_start(out=sp_x2[m][:, ta:ta + 512],
                                               in_=STRM(m, ta, ta + 512)))
        if _DEBUG_TAPS:
            _wt(Y, C.cv, C.cv.val)
            C.qs.inc(Y and Y.dma_start(out=taps["tap_x2"], in_=stream[:]))

    with nc.Block() as blk:
        run_walks(blk, phaseB)
    b_.close()
    ab.close()

    nB = {k: c.val for k, c in CTRS.items()}

    c1 = ExitStack()
    csb = lambda n, s, d: c1.enter_context(nc.sbuf_tensor(n, s, d))
    cpsf = lambda n, s: c1.enter_context(nc.psum_tensor(n, s, F32))
    xhat = csb("xhat", [128, KC * NT], BF16)
    xq32 = [csb("xq32_%d" % i, [128, 512], I32) for i in range(2)]
    wblk = [csb("wblk%d" % i, [128, KC * 128], BF16) for i in range(2)]
    fstg = [csb("fstg%d" % i, [128, NT], BF16) for i in range(2)]
    gtile = csb("gtile", [128, NT], BF16)
    uptile = csb("uptile", [128, NT], BF16)
    ptile = csb("ptile", [128, NT], BF16)
    optile = csb("optile", [128, NT], BF16)
    sigf = csb("sigf", [128, NT], F32)
    qi32 = csb("qi32", [128, NT], I32)
    sq32 = csb("sq32", [128, NT], I32)
    g232 = csb("g232", [128, NT], I32)
    u232 = csb("u232", [128, NT], I32)
    pf = [cpsf("pf%d" % i, [128, 512]) for i in range(4)]
    pcol = cpsf("pcol", [128, 16])
    prowC = c1.enter_context(nc.psum_tensor("prowC", [1, 128], BF16))

    def phaseC1(me, E):
        Y, S, V, T, G = engs(me, E)
        for X in (Y, S, V, T, G):
            if X:
                for k, v in nB.items():
                    X.wait_ge(CTRS[k].sem, v)

        if _DEBUG_TAPS:
            C.qs.inc(Y and Y.dma_start(out=taps["tap_mprt"], in_=mprt[:]))
        ng1 = finalize_and_allreduce(me, E, [((0, 64), 0)], "1")
        _wt(V, C.qg, ng1)
        nv_s1 = make_scale_cols(me, E, 0, 16, 17, pcol, 0)

        nv_x = 0
        for m in range(KC):
            nrd = C.qs.inc(Y and Y.dma_start(out=xhat[:, NT * m:NT * (m + 1)],
                                             in_=sp_h2[m]))
            for tt in range(8):
                sl = xhat[:, NT * m + 512 * tt: NT * m + 512 * (tt + 1)]
                _wt(V, C.qs, nrd)
                _wt(V, C.cv, nv_s1)
                V and V.tensor_scalar(xq32[tt % 2][:], sl, cols[:, 16:17], None,
                                      ALU.mult)
                nv_x = C.cv.inc(V and V.tensor_scalar(sl, xq32[tt % 2][:],
                                                      cols[:, 17:18], None, ALU.mult))
        if _DEBUG_TAPS:
            _wt(Y, C.cv, nv_x)
            C.qs.inc(Y and Y.dma_start(out=taps["tap_xhat"], in_=xhat[:]))

        stG = NS()
        stG.pf_free = [0, 0, 0, 0]
        stG.pf_free_cs = [0, 0, 0, 0]
        stG.stg_dma = [0, 0]
        stG.stg_red = [0, 0]
        stG.wblk_used = [0, 0]
        stG.last_e = ("v", 0)
        stG.nw = [0, 0]
        stG.nw[0] = C.qa.inc(S and S.dma_start(out=wblk[0][:], in_=sc_gt[0]))
        for i in range(IC):
            si = i % 2
            if i + 1 < IC:
                s2 = (i + 1) % 2
                _wt(S, C.ct, stG.wblk_used[s2])
                stG.nw[s2] = C.qa.inc(S and S.dma_start(out=wblk[s2][:],
                                                        in_=sc_gt[i + 1]))
            _wt(T, C.qa, stG.nw[si])
            _wt(T, C.cv, nv_x)
            nv_e = 0
            for t in range(8):
                b = t % 4
                _wt(T, C.cv, stG.pf_free[b])
                _wt(T, C.cs, stG.pf_free_cs[b])
                nt_f = 0
                for k in range(KC):
                    nt_f = C.ct.inc(T and T.matmul(
                        pf[b][:], wblk[si][:, 128 * k:128 * (k + 1)],
                        xhat[:, NT * k + 512 * t:NT * k + 512 * (t + 1)],
                        start=(k == 0), stop=(k == KC - 1), skip_group_check=True))
                if t == 0:
                    _wt(V, C.qs, stG.stg_dma[si])
                    _wt(S, C.qs, stG.stg_dma[si])
                    _wt(V, C.cg, stG.stg_red[si])
                    _wt(S, C.cg, stG.stg_red[si])
                if t % 2 == 0:
                    _wt(V, C.ct, nt_f)
                    nv_e = C.cv.inc(V and V.tensor_copy(
                        fstg[si][:, 512 * t:512 * (t + 1)], pf[b][:]))
                    stG.pf_free[b] = nv_e
                    stG.pf_free_cs[b] = 0
                    stG.last_e = ("v", nv_e)
                else:
                    _wt(S, C.ct, nt_f)
                    ns_e = C.cs.inc(S and S.copy(fstg[si][:, 512 * t:512 * (t + 1)],
                                                 pf[b][:]))
                    stG.pf_free[b] = 0
                    stG.pf_free_cs[b] = ns_e
                    stG.last_e = ("s", ns_e)
            stG.wblk_used[si] = C.ct.val
            _wt(Y, C.cv, nv_e)
            _wt(Y, C.cs, stG.last_e[1] if stG.last_e[0] == "s" else 0)
            stG.stg_dma[si] = C.qs.inc(Y and Y.dma_start(out=sp_gate[i],
                                                         in_=fstg[si][:]))
            _wt(V, C.cs, stG.last_e[1] if stG.last_e[0] == "s" else 0)
            _wt(G, C.cv, nv_e)
            _wt(G, C.cs, stG.last_e[1] if stG.last_e[0] == "s" else 0)
            stG.stg_red[si] = C.cg.inc(G and G.tensor_reduce(
                mprt[0:1, i:i + 1], fstg[si][:], AX.XYZWC, ALU.abs_max))

        ng3 = finalize_and_allreduce(me, E, [((0, 24), 1)], "3",
                                     wait_cg=max(stG.stg_red))
        _wt(V, C.qg, ng3)
        make_scale_cols(me, E, 1, 18, 19, pcol, 0, pcol_free=nv_s1)
        C.cv.inc(V and V.tensor_scalar(cols[:, 20:21], pcol[:, 1:2],
                                       2.0 ** -16, None, ALU.mult))
        nv_s3e = vpads(V)

        stU = NS()
        stU.pf_free = list(stG.pf_free)
        stU.pf_free_cs = list(stG.pf_free_cs)
        stU.stg_dma = list(stG.stg_dma)
        stU.stg_red = list(stG.stg_red)
        stU.wblk_used = list(stG.wblk_used)
        stU.last_e = stG.last_e
        f2 = NS()
        f2.p_dma = 0
        f2.q_done = 0
        f2.sq_done = 0
        f2.p_done = 0
        f2.red = 0
        stU.nw = [0, 0]
        _wt(S, C.ct, stU.wblk_used[0])
        stU.nw[0] = C.qa.inc(S and S.dma_start(out=wblk[0][:], in_=sc_up[0]))
        for i in range(IC):
            si = i % 2
            if i + 1 < IC:
                s2 = (i + 1) % 2
                _wt(S, C.ct, stU.wblk_used[s2])
                stU.nw[s2] = C.qa.inc(S and S.dma_start(out=wblk[s2][:],
                                                        in_=sc_up[i + 1]))
            _wt(S, C.cv, f2.p_done)
            _wt(S, C.cv, f2.sq_done)
            ng_t = C.qa.inc(S and S.dma_start(out=gtile[:], in_=sp_gate[i]))
            _wt(T, C.qa, stU.nw[si])
            nv_e = 0
            for t in range(8):
                b = t % 4
                _wt(T, C.cv, stU.pf_free[b])
                _wt(T, C.cs, stU.pf_free_cs[b])
                nt_f = 0
                for k in range(KC):
                    nt_f = C.ct.inc(T and T.matmul(
                        pf[b][:], wblk[si][:, 128 * k:128 * (k + 1)],
                        xhat[:, NT * k + 512 * t:NT * k + 512 * (t + 1)],
                        start=(k == 0), stop=(k == KC - 1), skip_group_check=True))
                if t == 0:
                    _wt(V, C.qs, stU.stg_dma[si])
                    _wt(S, C.qs, stU.stg_dma[si])
                    _wt(V, C.cg, stU.stg_red[si])
                    _wt(S, C.cg, stU.stg_red[si])
                if t % 2 == 0:
                    _wt(V, C.ct, nt_f)
                    nv_e = C.cv.inc(V and V.tensor_copy(
                        fstg[si][:, 512 * t:512 * (t + 1)], pf[b][:]))
                    stU.pf_free[b] = nv_e
                    stU.pf_free_cs[b] = 0
                    stU.last_e = ("v", nv_e)
                else:
                    _wt(S, C.ct, nt_f)
                    ns_e = C.cs.inc(S and S.copy(fstg[si][:, 512 * t:512 * (t + 1)],
                                                 pf[b][:]))
                    stU.pf_free[b] = 0
                    stU.pf_free_cs[b] = ns_e
                    stU.last_e = ("s", ns_e)
            stU.wblk_used[si] = C.ct.val
            _wt(Y, C.cv, nv_e)
            _wt(Y, C.cs, stU.last_e[1] if stU.last_e[0] == "s" else 0)
            stU.stg_dma[si] = C.qs.inc(Y and Y.dma_start(out=sp_up[i], in_=fstg[si][:]))
            _wt(V, C.cs, stU.last_e[1] if stU.last_e[0] == "s" else 0)
            _wt(G, C.cv, nv_e)
            _wt(G, C.cs, stU.last_e[1] if stU.last_e[0] == "s" else 0)
            stU.stg_red[si] = C.cg.inc(G and G.tensor_reduce(
                mprt[0:1, 24 + i:25 + i], fstg[si][:], AX.XYZWC, ALU.abs_max))
            _wt(S, C.qa, ng_t)
            _wt(S, C.cv, nv_s3e)
            S and S.activation(qi32[:], gtile[:], AF.Copy, scale=cols[:, 18:19])
            f2.q_done = C.cs.inc(S and S.activation(sigf[:], qi32[:], AF.Sigmoid,
                                                    scale=cols[:, 19:20]))
            _wt(V, C.cs, f2.q_done)
            f2.sq_done = C.cv.inc(V and V.tensor_scalar(sq32[:], sigf[:], 65536.0,
                                                        65535.0, ALU.mult, ALU.min))
            _wt(V, C.qs, f2.p_dma)
            _wt(V, C.cg, f2.red)
            f2.p_done = C.cv.inc(V and V.scalar_tensor_tensor(
                ptile[:], qi32[:], cols[:, 20:21], sq32[:], ALU.mult, ALU.mult))
            _wt(Y, C.cv, f2.p_done)
            f2.p_dma = C.qs.inc(Y and Y.dma_start(out=sp_p[i], in_=ptile[:]))
            _wt(G, C.cv, f2.p_done)
            f2.red = C.cg.inc(G and G.tensor_reduce(
                mprt[0:1, 48 + i:49 + i], ptile[:], AX.XYZWC, ALU.abs_max))

        ng24 = finalize_and_allreduce(me, E, [((24, 48), 2), ((48, 72), 3)], "24",
                                      wait_cg=max(max(stU.stg_red), f2.red))
        _wt(V, C.qg, ng24)
        make_scale_cols(me, E, 2, 21, 22, pcol, 0, pcol_free=nv_s3e)
        nv_mk4 = make_scale_cols(me, E, 3, 23, 24, pcol, 0, pcol_free=nv_s3e)
        C.cv.inc(V and V.tensor_tensor(cols[:, 25:26], pcol[:, 1:2],
                                       cols[:, 22:23], ALU.mult))
        nv_s42 = vpads(V)

        fb = NS()
        fb.o_dma = 0
        fb.g2 = 0
        fb.u2 = 0
        fb.op = 0
        fb.red = 0
        for i in range(IC):
            _wt(Y, C.cv, fb.g2)
            nrp = C.qs.inc(Y and Y.dma_start(out=ptile[:], in_=sp_p[i]))
            _wt(S, C.cv, fb.op)
            nru = C.qa.inc(S and S.dma_start(out=uptile[:], in_=sp_up[i]))
            _wt(V, C.qs, nrp)
            _wt(V, C.cv, nv_s42)
            fb.g2 = C.cv.inc(V and V.tensor_scalar(g232[:], ptile[:], cols[:, 23:24],
                                                   None, ALU.mult))
            _wt(S, C.qa, nru)
            _wt(S, C.cv, nv_s42)
            fb.u2 = C.cs.inc(S and S.activation(u232[:], uptile[:], AF.Copy,
                                                scale=cols[:, 21:22]))
            _wt(V, C.cs, fb.u2)
            _wt(V, C.qs, fb.o_dma)
            _wt(V, C.cg, fb.red)
            fb.op = C.cv.inc(V and V.scalar_tensor_tensor(
                optile[:], g232[:], cols[:, 25:26], u232[:], ALU.mult, ALU.mult))
            _wt(Y, C.cv, fb.op)
            fb.o_dma = C.qs.inc(Y and Y.dma_start(out=sp_o[i], in_=optile[:]))
            _wt(G, C.cv, fb.op)
            fb.red = C.cg.inc(G and G.tensor_reduce(
                mprt[0:1, 72 + i:73 + i], optile[:], AX.XYZWC, ALU.abs_max))

        ng5 = finalize_and_allreduce(me, E, [((72, 96), 4)], "5", wait_cg=fb.red)
        _wt(V, C.qg, ng5)
        make_scale_cols(me, E, 4, 26, 27, pcol, 0, pcol_free=nv_s42)
        if _DEBUG_TAPS:
            _wt(Y, C.cv, C.cv.val)
            C.qs.inc(Y and Y.dma_start(out=taps["tap_scal"], in_=cols[:]))
            C.qs.inc(Y and Y.dma_start(out=taps["tap_mrow"], in_=mrow[:]))

    with nc.Block() as blk:
        run_walks(blk, phaseC1)
    c1.close()

    nC1 = {k: c.val for k, c in CTRS.items()}

    c2 = ExitStack()
    c2sb = lambda n, s, d: c2.enter_context(nc.sbuf_tensor(n, s, d))
    c2ps = lambda n, s: c2.enter_context(nc.psum_tensor(n, s, F32))
    wdnall = c2sb("wdnall", [128, KC * IC * 128], BF16)
    ohat = [c2sb("ohat%d" % i, [128, IC * 512], BF16) for i in range(2)]
    oq32 = [c2sb("oq32_%d" % i, [128, 512], I32) for i in range(2)]
    opin = [c2sb("opin%d" % i, [128, 512], BF16) for i in range(2)]
    x2in = [c2sb("x2in%d" % i, [128, 512], BF16) for i in range(2)]
    ostg = [c2sb("ostg%d" % i, [128, 512], F32) for i in range(2)]
    pd = [c2ps("pd%d" % i, [128, 512]) for i in range(4)]

    def phaseC2(me, E):
        Y, S, V, T, G = engs(me, E)
        for X in (Y, S, V, T, G):
            if X:
                for k, v in nC1.items():
                    X.wait_ge(CTRS[k].sem, v)
        nwd = 0
        for m in range(KC):
            nwd = C.qa.inc(S and S.dma_start(
                out=wdnall[:, IC * 128 * m:IC * 128 * (m + 1)], in_=sc_dn[m]))
        stD = NS()
        stD.oh_used = [0, 0]
        stD.pd_free = [0, 0, 0, 0]
        stD.out_dma = [0, 0]
        stD.opin_free = [0, 0]
        stD.x2_used = [0, 0]
        stD.nv_oh = [0, 0]
        nfinal = 0

        def stage(t2, j0=0, j1=IC):
            ta2 = 512 * t2
            ti2 = t2 % 2
            nv_oh = 0
            for j in range(j0, j1):
                jj = j % 2
                _wt(Y, C.cv, stD.opin_free[jj])
                nro = C.qs.inc(Y and Y.dma_start(out=opin[jj][:],
                                                 in_=sp_o[j][:, ta2:ta2 + 512]))
                _wt(V, C.qs, nro)
                if j < 2:
                    _wt(V, C.ct, stD.oh_used[ti2])
                V and V.tensor_scalar(oq32[jj][:], opin[jj][:], cols[:, 26:27],
                                      None, ALU.mult)
                nv_oh = C.cv.inc(V and V.tensor_scalar(
                    ohat[ti2][:, 512 * j:512 * (j + 1)], oq32[jj][:], cols[:, 27:28],
                    None, ALU.mult))
                stD.opin_free[jj] = nv_oh
            if j1 == IC:
                stD.nv_oh[ti2] = nv_oh

        stage(0)
        for t in range(8):
            ta = 512 * t
            ti = t % 2
            if t + 1 < 8:
                stage(t + 1, 0, 12)
            _wt(T, C.cv, stD.nv_oh[ti])
            _wt(T, C.qa, nwd)
            for m in range(KC):
                if m == 4 and t + 1 < 8:
                    stage(t + 1, 12, IC)
                b = m % 4
                _wt(T, C.cv, stD.pd_free[b])
                nt_d = 0
                for j in range(IC):
                    nt_d = C.ct.inc(T and T.matmul(
                        pd[b][:],
                        wdnall[:, IC * 128 * m + 128 * j:IC * 128 * m + 128 * (j + 1)],
                        ohat[ti][:, 512 * j:512 * (j + 1)],
                        start=(j == 0), stop=(j == IC - 1), skip_group_check=True))
                stD.oh_used[ti] = nt_d
                mi = m % 2
                _wt(S, C.cv, stD.x2_used[mi])
                nx2 = C.qa.inc(S and S.dma_start(out=x2in[mi][:],
                                                 in_=sp_x2[m][:, ta:ta + 512]))
                _wt(V, C.ct, nt_d)
                _wt(V, C.qa, nx2)
                _wt(V, C.qa, stD.out_dma[mi])
                nv_o = C.cv.inc(V and V.scalar_tensor_tensor(
                    ostg[mi][:], pd[b][:], cols[:, 8 + m:9 + m], x2in[mi][:],
                    ALU.mult, ALU.add))
                stD.pd_free[b] = nv_o
                stD.x2_used[mi] = nv_o
                _wt(S, C.cv, nv_o)
                nfinal = C.qa.inc(S and S.dma_start(
                    out=out_fm[128 * m:128 * (m + 1), ta:ta + 512], in_=ostg[mi][:]))
                stD.out_dma[mi] = nfinal
        for X in (Y, S, V, T, G):
            X and X.wait_ge(C.qa.sem, nfinal)

    with nc.Block() as blk:
        run_walks(blk, phaseC2)
    c2.close()
    glob.close()

    return nc


_NC_CACHE = {}
_LAST_RES = None


def _get_nc():
    key = "taps" if _DEBUG_TAPS else "plain"
    if key not in _NC_CACHE:
        _NC_CACHE[key] = build_program()
    return _NC_CACHE[key]


def kernel(L0_attn_x, L0_attn_Wq, L0_attn_Wk, L0_attn_Wv, L0_attn_Wo,
           L0_attn_fmq_w, L0_attn_fmq_b, L0_attn_fmk_w, L0_attn_fmk_b,
           L0_attn_rms_w, L0_ffn_w_gate, L0_ffn_w_up, L0_ffn_w_down,
           L0_c, L0_wmod):
    x = np.asarray(L0_attn_x, np.float32)
    common = {
        "wqT": np.ascontiguousarray(np.asarray(L0_attn_Wq, np.float32).T),
        "wkT": np.ascontiguousarray(np.asarray(L0_attn_Wk, np.float32).T),
        "wvT": np.ascontiguousarray(np.asarray(L0_attn_Wv, np.float32).T),
        "woT": np.ascontiguousarray(np.asarray(L0_attn_Wo, np.float32).T),
        "fmqT": np.ascontiguousarray(np.asarray(L0_attn_fmq_w, np.float32).T),
        "fmkT": np.ascontiguousarray(np.asarray(L0_attn_fmk_w, np.float32).T),
        "fmq_b": np.ascontiguousarray(np.asarray(L0_attn_fmq_b, np.float32)[:, None]),
        "fmk_b": np.ascontiguousarray(np.asarray(L0_attn_fmk_b, np.float32)[:, None]),
        "fmk_b_row": np.ascontiguousarray(np.asarray(L0_attn_fmk_b, np.float32)[None, :]),
        "rms_row": np.ascontiguousarray(np.asarray(L0_attn_rms_w, np.float32)[None, :]),
        "wmod": np.ascontiguousarray(np.asarray(L0_wmod, np.float32)),
        "w_up": np.ascontiguousarray(np.asarray(L0_ffn_w_up, np.float32)),
        "w_gate": np.ascontiguousarray(np.asarray(L0_ffn_w_gate, np.float32)),
        "w_down": np.ascontiguousarray(np.asarray(L0_ffn_w_down, np.float32)),
    }
    cmat = np.asarray(L0_c, np.float32)
    in_maps = []
    for c in range(8):
        m = dict(common)
        m["xT"] = np.ascontiguousarray(x[c].T)
        m["c_row"] = np.ascontiguousarray(cmat[c:c + 1])
        in_maps.append(m)
    nc = _get_nc()
    res = run_bass_kernel_spmd(nc, in_maps, core_ids=list(range(8)))
    global _LAST_RES
    _LAST_RES = res
    out = np.stack([np.asarray(res.results[c]["out_fm"]).T for c in range(8)])
    return np.ascontiguousarray(out, dtype=np.float32)
